# version 01
# baseline (speedup 1.0000x reference)
import contextlib
import numpy as np
import concourse.bass as bass
import concourse.mybir as mybir
from concourse.bass_utils import run_bass_kernel_spmd

F32 = mybir.dt.float32
BF16 = mybir.dt.bfloat16
I32 = mybir.dt.int32
AF = mybir.ActivationFunctionType
ALU = mybir.AluOpType
AX = mybir.AxisListType

NCORES = 8
D = 1024
DIN = 2048
DST = 16
DTR = 64
DFF = 4096
DI4 = 2816
B = 2
L = 8192
LC = L // NCORES
T = B * LC
SEG = LC + 3
TH = B * SEG
EPS = 1e-5
NT = T // 128


class Prog:

    ENGS = ("sync", "scalar", "vector", "tensor", "gpsimd", "gdma")
    QUEUE = {"sync": "sync", "scalar": "scalar", "vector": "vector",
             "tensor": "tensor", "gpsimd": "gpsimd", "gdma": "gpsimd"}
    QUEUES = ("sync", "scalar", "vector", "tensor", "gpsimd")

    def __init__(self, nc, es):
        self.nc = nc
        self.es = es
        self.ops = {q: [] for q in self.QUEUES}
        self.chain = {}
        for e in self.ENGS:
            h = es.enter_context(nc.semaphore("ch_" + e))
            self.chain[e] = [h, 0]
        self.wm = {q: {e2: 0 for e2 in self.ENGS} for q in self.QUEUES}

    def tok(self, eng):
        return (eng, self.chain[eng][1])

    def _emit_wait(self, q, e2, n):
        if n <= 0 or n <= self.wm[q][e2]:
            return
        self.wm[q][e2] = n
        h = self.chain[e2][0]
        self.ops[q].append(lambda e, h=h, n=n: e.wait_ge(h, n))

    def op(self, eng, build, inc=1, serial=True, after=()):
        q = self.QUEUE[eng]
        ch = self.chain[eng]
        dma = eng in ("sync", "gdma")
        if dma:
            inc = 16
            serial = True
        if serial and ch[1] > 0:
            self._emit_wait(q, eng, ch[1])
        best = {}
        for t_ in after:
            if t_ is None:
                continue
            e2, n = t_
            best[e2] = max(best.get(e2, 0), n)
        for e2, n in best.items():
            if e2 == eng and serial:
                continue
            self._emit_wait(q, e2, n)
        h = ch[0]
        def emit(e, build=build, h=h, inc=inc):
            build(e).then_inc(h, inc)
        self.ops[q].append(emit)
        ch[1] += inc
        return (eng, ch[1])

    def barrier_wait(self, eng, after):
        q = self.QUEUE[eng]
        best = {}
        for t_ in after:
            if t_ is None:
                continue
            e2, n = t_
            best[e2] = max(best.get(e2, 0), n)
        for e2, n in best.items():
            self._emit_wait(q, e2, n)

    def finalize(self):
        block = self.es.enter_context(self.nc.Block())
        for q in self.QUEUES:
            ops = self.ops[q]
            if not ops:
                continue
            dec = getattr(block, q)
            def section(e, ops=ops):
                for f in ops:
                    f(e)
            dec(section)


def sb(nc, es, name, shape, dtype=F32):
    return es.enter_context(nc.sbuf_tensor(name, shape, dtype))


def ps(nc, es, name, shape, dtype=F32):
    return es.enter_context(nc.psum_tensor(name, shape, dtype))


def _rep16(handle, nrows, c0, nblk, rep=16):
    pitch = int(handle.shape[-1])
    return bass.AP(handle, c0, [[pitch, nrows], [1, nblk], [0, rep]])


def emit_ln_stats(p, tiles, stat, scr, pre=()):
    NTI = len(tiles)
    C = lambda k: stat[:, k * NTI:(k + 1) * NTI]
    M, RSTD, SA, SQ, MSQ, Q, VAR, LNV, NH = (C(k) for k in range(9))
    t = None
    for i, (src, nrows, src_tok) in enumerate(tiles):
        p.op("scalar", lambda s, src=src, nrows=nrows, i=i:
             s.activation(scr[0:nrows, :], src, AF.Identity,
                          accum_out=SA[0:nrows, i:i + 1]),
             after=[src_tok] + list(pre))
        t = p.op("scalar", lambda s, src=src, nrows=nrows, i=i:
                 s.activation(scr[0:nrows, :], src, AF.Square,
                              accum_out=SQ[0:nrows, i:i + 1]),
                 after=[src_tok])
    tm = p.op("vector", lambda v: v.tensor_scalar(M, SA, 1.0 / D, None,
                                                  ALU.mult), after=[t])
    tq = p.op("vector", lambda v: v.tensor_scalar(Q, SQ, 1.0 / D, EPS,
                                                  ALU.mult, ALU.add))
    tmsq = p.op("scalar", lambda s: s.activation(MSQ, M, AF.Square),
                after=[tm])
    tvar = p.op("gpsimd", lambda g: g.tensor_tensor(VAR, Q, MSQ, ALU.subtract),
                after=[tq, tmsq])
    tln = p.op("scalar", lambda s: s.activation(LNV, VAR, AF.Ln), after=[tvar])
    tnh = p.op("vector", lambda v: v.tensor_scalar(NH, LNV, -0.5, None,
                                                   ALU.mult), after=[tln])
    tr = p.op("scalar", lambda s: s.activation(RSTD, NH, AF.Exp), after=[tnh])
    return M, RSTD, tr



def build_l0pre():
    nc = bass.Bass("TRN2", target_bir_lowering=False, debug=False)
    es = contextlib.ExitStack()
    p = Prog(nc, es)

    x_d = nc.dram_tensor("x", [T, D], F32, kind="ExternalInput").ap()
    xh_d = nc.dram_tensor("xhalo", [8, D], F32, kind="ExternalInput").ap()
    hm_d = nc.dram_tensor("hmask", [128, 1], F32, kind="ExternalInput").ap()
    g1_d = nc.dram_tensor("ln1_g", [1, D], F32, kind="ExternalInput").ap()
    b1_d = nc.dram_tensor("ln1_b", [1, D], F32, kind="ExternalInput").ap()
    w_d = nc.dram_tensor("in_proj_w", [D, 2 * DIN], F32, kind="ExternalInput").ap()
    cw_d = nc.dram_tensor("conv_w", [DIN, 4], F32, kind="ExternalInput").ap()
    cb_d = nc.dram_tensor("conv_b", [1, DIN], F32, kind="ExternalInput").ap()
    xp_d = nc.dram_tensor("x_proj_w", [DIN, 96], F32, kind="ExternalInput").ap()
    dw_d = nc.dram_tensor("dt_proj_w", [DTR, DIN], F32, kind="ExternalInput").ap()
    db_d = nc.dram_tensor("dt_proj_b", [1, DIN], F32, kind="ExternalInput").ap()
    idn_d = nc.dram_tensor("iden", [128, 128], F32, kind="ExternalInput").ap()

    xn_d = nc.dram_tensor("xn", [T, D], F32, kind="ExternalOutput").ap()
    xcT_d = nc.dram_tensor("xcT", [DIN, T], F32, kind="ExternalOutput").ap()
    zT_d = nc.dram_tensor("zT", [DIN, T], F32, kind="ExternalOutput").ap()
    dtT_d = nc.dram_tensor("dtT", [DIN, T], F32, kind="ExternalOutput").ap()
    bc_d = nc.dram_tensor("BC", [32, T], F32, kind="ExternalOutput").ap()
    xcT_i = nc.dram_tensor("xcT_i", [DIN, T], F32, kind="Internal").ap()

    xtok = sb(nc, es, "xtok", [128, NT * D])
    xhalo = sb(nc, es, "xhalo_t", [8, D])
    xnh = sb(nc, es, "xnh", [8, D])
    hmask = sb(nc, es, "hmask_t", [128, 1])
    stat = sb(nc, es, "stat", [128, 9 * 17])
    stat2 = sb(nc, es, "stat2", [128, 9 * 17])
    scr = sb(nc, es, "scr", [128, D])
    xn1pp = [sb(nc, es, f"xn1pp{j}", [128, D]) for j in range(2)]
    g1row = sb(nc, es, "g1row", [1, D])
    g1b = sb(nc, es, "g1b", [128, D])
    b1b = sb(nc, es, "b1b", [128, D])
    iden = sb(nc, es, "iden_t", [128, 128])
    one1 = sb(nc, es, "one1", [1, 128])
    xn1T = sb(nc, es, "xn1T", [128, 8 * TH], BF16)
    wblk = [sb(nc, es, f"wblk{j}", [128, 8 * 128]) for j in range(2)]
    wblkb = [sb(nc, es, f"wblkb{j}", [128, 8 * 128], BF16) for j in range(2)]
    xi = [sb(nc, es, f"xi{j}", [128, TH]) for j in range(2)]
    acc = [sb(nc, es, f"acc{j}", [128, T]) for j in range(2)]
    cwt = sb(nc, es, "cwt", [128, 16 * 4])
    cbrow = sb(nc, es, "cbrow", [1, DIN])
    cbcols = sb(nc, es, "cbcols", [128, 16])
    dbcols = sb(nc, es, "dbcols", [128, 16])
    xpw = sb(nc, es, "xpw", [128, 16 * 96])
    dwwb = [sb(nc, es, f"dwwb{j}", [64, 128]) for j in range(2)]
    xdbl = sb(nc, es, "xdbl", [96, T])
    xcload = [sb(nc, es, f"xcload{j}", [128, 512]) for j in range(2)]
    dtt = [sb(nc, es, f"dtt{j}", [128, 512]) for j in range(2)]

    psA = ps(nc, es, "psA", [128, 512])
    psB = ps(nc, es, "psB", [128, 512])
    psT = [ps(nc, es, f"psT{j}", [128, 512]) for j in range(2)]
    psW = ps(nc, es, "psW", [128, 512])
    psX = ps(nc, es, "psX", [96, 512])

    xtok3 = xtok[:].rearrange("p (i d) -> p i d", d=D)
    xn1T3 = xn1T[:].rearrange("p (k t) -> p k t", t=TH)

    DMA = lambda out, in_: (lambda s: s.dma_start(out, in_))

    t_x = p.op("sync", DMA(xtok3, x_d.rearrange("(i p) d -> p i d", p=128)),
               inc=16, serial=False)
    t_xh = p.op("sync", DMA(xhalo[0:6, :], xh_d[0:6, :]), inc=16, serial=False)
    t_hm = p.op("sync", DMA(hmask[:], hm_d[:]), inc=16, serial=False)
    t_g1 = p.op("sync", DMA(g1row[:], g1_d[:]), inc=16, serial=False)
    t_idn = p.op("sync", DMA(iden[:], idn_d[:]), inc=16, serial=False)
    t_cw = p.op("sync", DMA(cwt[:].rearrange("p (i k) -> p i k", k=4),
                            cw_d.rearrange("(i p) k -> p i k", p=128)),
                inc=16, serial=False)
    t_cb = p.op("sync", DMA(cbrow[:], cb_d[:]), inc=16, serial=False)
    t_xp = p.op("sync", DMA(xpw[:].rearrange("p (i n) -> p i n", n=96),
                            xp_d.rearrange("(i p) n -> p i n", p=128)),
                inc=16, serial=False)

    t_one = p.op("vector", lambda v: v.memset(one1[:], 1.0), serial=False)
    t_st1 = p.op("vector", lambda v: v.memset(stat[:], 1.0), serial=False)
    t_st2 = p.op("vector", lambda v: v.memset(stat2[:], 1.0), serial=False)

    def row_bcast(row_ap, dst, row_tok, extra=()):
        ev = None
        for h in range(2):
            a = [t_one, row_tok] + list(extra)
            if ev is not None:
                a.append(ev)
            tpe = p.op("tensor", lambda t, h=h: t.matmul(
                psW[:, 0:512], one1[0:1, 0:128],
                row_ap[0:1, 512 * h:512 * (h + 1)], start=True, stop=True),
                serial=False, after=a)
            ev = p.op("scalar", lambda s, h=h: s.copy(
                dst[:, 512 * h:512 * (h + 1)], psW[:, 0:512]), after=[tpe])
        return ev

    t_g1b = row_bcast(g1row, g1b, t_g1)
    t_b1 = p.op("sync", DMA(g1row[:], b1_d[:]), inc=16, serial=False,
                after=[t_g1b])
    t_b1b = row_bcast(g1row, b1b, t_b1, extra=[t_g1b])

    def col_load(row_ap, dst, row_tok, extra=()):
        tk = None
        for i in range(16):
            tk = p.op("tensor", lambda t, i=i: t.matmul(
                psW[:, i:i + 1], row_ap[0:1, 128 * i:128 * (i + 1)],
                one1[0:1, 0:1], start=True, stop=True), serial=False,
                after=[t_one, row_tok] + list(extra) if i == 0 else ())
        return p.op("scalar", lambda s: s.copy(dst[:], psW[:, 0:16]),
                    after=[tk])

    t_cbc = col_load(cbrow, cbcols, t_cb, extra=[t_b1b])
    t_db = p.op("sync", DMA(cbrow[:], db_d[:]), inc=16, serial=False,
                after=[t_cbc])
    t_dbc = col_load(cbrow, dbcols, t_db, extra=[t_cbc])

    tilesA = [(xtok3[:, i, :], 128, t_x) for i in range(NT)] + \
             [(xhalo[0:6, :], 6, t_xh)]
    M0, R0, tR0 = emit_ln_stats(p, tilesA, stat, scr, pre=[t_st1, t_st2])
    ap_toks = []
    st_xn_toks = []
    for i in range(NT):
        ta = p.op("vector", lambda v, i=i: v.tensor_scalar(
            xtok3[:, i, :], xtok3[:, i, :], M0[:, i:i + 1], R0[:, i:i + 1],
            ALU.subtract, ALU.mult), after=[tR0])
        ap_toks.append(ta)
        st_xn_toks.append(p.op("gdma", DMA(
            xn_d[128 * i:128 * (i + 1), :], xtok3[:, i, :]), inc=16,
            serial=False, after=[ta]))
    ta_h = p.op("vector", lambda v: v.tensor_scalar(
        xnh[0:6, :], xhalo[0:6, :], M0[0:6, NT:NT + 1], R0[0:6, NT:NT + 1],
        ALU.subtract, ALU.mult), after=[tR0])
    ap_toks.append(ta_h)

    tilesB = [(xtok3[:, i, :], 128, ap_toks[i]) for i in range(NT)] + \
             [(xnh[0:6, :], 6, ta_h)]
    M1, R1, tR1 = emit_ln_stats(p, tilesB, stat2, scr)

    tp_toks = []
    ev_toks = []
    for i in range(NT + 1):
        nrows = 128 if i < NT else 6
        src_ap = xtok3[:, i, :] if i < NT else xnh[0:6, :]
        xn1buf = xn1pp[i % 2]
        a = [tR1, t_g1b, t_b1b]
        if i >= 2:
            a.append(tp_toks[8 * (i - 1) - 1])
        p.op("vector", lambda v, src_ap=src_ap, xn1buf=xn1buf,
             nrows=nrows, i=i: v.tensor_scalar(
                 xn1buf[0:nrows, :], src_ap, M1[0:nrows, i:i + 1],
                 R1[0:nrows, i:i + 1], ALU.subtract, ALU.mult), after=a)
        p.op("vector", lambda v, xn1buf=xn1buf, nrows=nrows:
             v.tensor_tensor(xn1buf[0:nrows, :], xn1buf[0:nrows, :],
                             g1b[0:nrows, :], ALU.mult))
        t3 = p.op("vector", lambda v, xn1buf=xn1buf, nrows=nrows:
                  v.tensor_tensor(xn1buf[0:nrows, :], xn1buf[0:nrows, :],
                                  b1b[0:nrows, :], ALU.add))
        for k in range(8):
            n_tp = len(tp_toks)
            j = n_tp % 2
            a = [t3, t_idn]
            if n_tp >= 2:
                a.append(ev_toks[n_tp - 2])
            if i < NT:
                tt = p.op("tensor", lambda t, k=k, j=j, xn1buf=xn1buf:
                          t.transpose(psT[j][:, 0:128],
                                      xn1buf[:, 128 * k:128 * (k + 1)],
                                      iden[:]), serial=False, after=a)
                base = 3 + 128 * i if i < 8 else SEG + 3 + 128 * (i - 8)
                te = p.op("scalar", lambda s, k=k, base=base, j=j: s.copy(
                    xn1T3[:, k, base:base + 128], psT[j][:, 0:128]),
                    after=[tt])
            else:
                tt = p.op("tensor", lambda t, k=k, j=j, xn1buf=xn1buf:
                          t.transpose(psT[j][:, 0:6],
                                      xn1buf[0:6, 128 * k:128 * (k + 1)],
                                      iden[0:6, 0:6]), serial=False, after=a)
                def ev_halo(s, k=k, j=j):
                    s.copy(xn1T3[:, k, 0:3], psT[j][:, 0:3])
                    return s.copy(xn1T3[:, k, SEG:SEG + 3], psT[j][:, 3:6])
                te = p.op("scalar", ev_halo, inc=2, after=[tt])
            tp_toks.append(tt)
            ev_toks.append(te)

    t_hm_done = None
    for k in range(8):
        p.op("vector", lambda v, k=k: v.tensor_scalar(
            xn1T3[:, k, 0:3], xn1T3[:, k, 0:3], hmask[:], None, ALU.mult),
            after=[ev_toks[-1], t_hm])
        t_hm_done = p.op("vector", lambda v, k=k: v.tensor_scalar(
            xn1T3[:, k, SEG:SEG + 3], xn1T3[:, k, SEG:SEG + 3], hmask[:],
            None, ALU.mult))

    NCH = [(0, 512), (512, 512), (1024, 512), (1536, 512), (2048, 6)]
    mm_last = [None] * 32
    cv_tok = [None] * 32
    ev_hist = []
    xi_free = [None] * 32
    st_xc_tok = [None] * 32

    for m in range(32):
        j = m % 2
        a = []
        if m >= 2:
            a.append(cv_tok[m - 2])
        tld = p.op("sync", lambda s, m=m, j=j: s.dma_start(
            wblk[j][:].rearrange("p (k m) -> p k m", m=128),
            w_d[:, 128 * m:128 * (m + 1)].rearrange("(k p) m -> p k m", p=128)),
            inc=16, serial=False, after=a)
        a = [tld]
        if m >= 2:
            a.append(mm_last[m - 2])
        cv_tok[m] = p.op("vector", lambda v, j=j: v.tensor_copy(
            wblkb[j][:], wblk[j][:]), after=a)
        wv = wblkb[j][:].rearrange("p (k m) -> p k m", m=128)
        for ci, (c0, cn) in enumerate(NCH):
            g = m * 5 + ci
            pp = psA if g % 2 == 0 else psB
            a = [cv_tok[m], t_hm_done]
            if g >= 2:
                a.append(ev_hist[g - 2])
            if m >= 2 and ci == 0:
                a.append(xi_free[m - 2])
            tk = None
            for k in range(8):
                tk = p.op("tensor", lambda t, k=k, c0=c0, cn=cn, pp=pp, wv=wv:
                          t.matmul(pp[:, 0:cn], wv[:, k, :],
                                   xn1T3[:, k, c0:c0 + cn],
                                   start=(k == 0), stop=(k == 7)),
                          serial=False, after=a if k == 0 else ())
            mm_last[m] = tk
            tev = p.op("scalar", lambda s, c0=c0, cn=cn, pp=pp, j=j: s.copy(
                xi[j][:, c0:c0 + cn], pp[:, 0:cn]), after=[tk])
            ev_hist.append(tev)
        if m < 16:
            xiv = xi[j][:].rearrange("p (s t) -> p s t", t=SEG)
            a0t, a1t = acc[0], acc[1]
            wk = cwt[:].rearrange("p (i k) -> p i k", k=4)
            a = [ev_hist[-1], t_cw]
            if m >= 1:
                a.append(st_xc_tok[m - 1])
            p.op("vector", lambda v, xiv=xiv, m=m, wk=wk, a0t=a0t:
                 v.tensor_scalar(a0t[:], xiv[:, :, 0:LC], wk[:, m, 0:1],
                                 None, ALU.mult), after=a)
            p.op("vector", lambda v, xiv=xiv, m=m, wk=wk, a0t=a0t, a1t=a1t:
                 v.scalar_tensor_tensor(a1t[:], xiv[:, :, 1:1 + LC],
                                        wk[:, m, 1:2], a0t[:], ALU.mult,
                                        ALU.add))
            p.op("vector", lambda v, xiv=xiv, m=m, wk=wk, a0t=a0t, a1t=a1t:
                 v.scalar_tensor_tensor(a0t[:], xiv[:, :, 2:2 + LC],
                                        wk[:, m, 2:3], a1t[:], ALU.mult,
                                        ALU.add))
            c4 = p.op("vector", lambda v, xiv=xiv, m=m, wk=wk, a0t=a0t,
                      a1t=a1t: v.scalar_tensor_tensor(
                          a1t[:], xiv[:, :, 3:3 + LC], wk[:, m, 3:4], a0t[:],
                          ALU.mult, ALU.add))
            xi_free[m] = c4
            t_si = p.op("scalar", lambda s, m=m, a0t=a0t, a1t=a1t:
                        s.activation(a0t[:], a1t[:], AF.Silu,
                                     bias=cbcols[:, m:m + 1]),
                        after=[c4, t_cbc])
            p.op("gdma", lambda g_, m=m, a0t=a0t: g_.dma_start(
                xcT_d[128 * m:128 * (m + 1), :], a0t[:]), inc=16,
                serial=False, after=[t_si])
            st_xc_tok[m] = p.op("gdma", lambda g_, m=m, a0t=a0t:
                                g_.dma_start(
                                    xcT_i[128 * m:128 * (m + 1), :], a0t[:]))
        else:
            xiv = xi[j][:].rearrange("p (s t) -> p s t", t=SEG)
            st_z = p.op("gdma", lambda g_, m=m, xiv=xiv: g_.dma_start(
                zT_d[128 * (m - 16):128 * (m - 15), :].rearrange(
                    "p (s t) -> p s t", t=LC), xiv[:, :, 3:3 + LC]), inc=16,
                serial=False, after=[ev_hist[-1]])
            st_xc_tok[m] = st_z
            xi_free[m] = st_z

    xpw3 = xpw[:].rearrange("p (i n) -> p i n", n=96)
    xd_ev = []
    mm_hist = []
    for ci in range(4):
        c0 = 512 * ci
        for k in range(16):
            gidx = ci * 16 + k
            jj = gidx % 2
            a = [st_xc_tok[k]]
            if gidx >= 2:
                a.append(mm_hist[gidx - 2])
            tl = p.op("sync", lambda s, k=k, c0=c0, jj=jj: s.dma_start(
                xcload[jj][:], xcT_i[128 * k:128 * (k + 1), c0:c0 + 512]),
                inc=16, serial=False, after=a)
            a2 = [tl, t_xp]
            if ci > 0 and k == 0:
                a2.append(xd_ev[-1])
            tm_ = p.op("tensor", lambda t, k=k, jj=jj: t.matmul(
                psX[:, 0:512], xpw3[:, k, :], xcload[jj][:],
                start=(k == 0), stop=(k == 15)), serial=False, after=a2)
            mm_hist.append(tm_)
        xd_ev.append(p.op("scalar", lambda s, c0=c0: s.copy(
            xdbl[:, c0:c0 + 512], psX[:, 0:512]), after=[mm_hist[-1]]))
    t_bc = p.op("gdma", DMA(bc_d[:], xdbl[64:96, :]), inc=16, serial=False,
                after=[xd_ev[-1]])

    sp1 = scr[:, 0:512]
    sp2 = scr[:, 512:1024]
    dt_mm = [None] * 64
    dt_abs = [None] * 64
    dt_v = [None] * 64
    dt_st = [None] * 64
    for m in range(16):
        jw = m % 2
        a = [xd_ev[-1]]
        if m >= 2:
            a.append(dt_mm[4 * (m - 1) - 1])
        tldw = p.op("sync", lambda s, m=m, jw=jw: s.dma_start(
            dwwb[jw][:], dw_d[:, 128 * m:128 * (m + 1)]), inc=16,
            serial=False, after=a)
        for ci in range(4):
            g = m * 4 + ci
            c0 = 512 * ci
            pp = psA if g % 2 == 0 else psB
            a = [tldw, xd_ev[ci], ev_hist[-1]]
            if g >= 2:
                a.append(dt_abs[g - 2])
            dt_mm[g] = p.op("tensor", lambda t, jw=jw, c0=c0, pp=pp: t.matmul(
                pp[:, 0:512], dwwb[jw][:], xdbl[0:64, c0:c0 + 512],
                start=True, stop=True), serial=False, after=a)
            a = [dt_mm[g], t_dbc]
            if g >= 1:
                a.append(dt_v[g - 1])
            p.op("scalar", lambda s, m=m, pp=pp: s.activation(
                sp1, pp[:, 0:512], AF.Relu, bias=dbcols[:, m:m + 1]), after=a)
            dt_abs[g] = p.op("scalar", lambda s, m=m, pp=pp: s.activation(
                sp2, pp[:, 0:512], AF.Abs, bias=dbcols[:, m:m + 1]))
            p.op("scalar", lambda s: s.activation(sp2, sp2, AF.Exp,
                                                  scale=-1.0))
            tln = p.op("scalar", lambda s: s.activation(sp2, sp2, AF.Ln,
                                                        bias=1.0))
            a = [tln]
            if g >= 2:
                a.append(dt_st[g - 2])
            dt_v[g] = p.op("vector", lambda v, g=g: v.tensor_tensor(
                dtt[g % 2][:], sp1, sp2, ALU.add), after=a)
            dt_st[g] = p.op("gdma", lambda g_, m=m, c0=c0, g=g: g_.dma_start(
                dtT_d[128 * m:128 * (m + 1), c0:c0 + 512], dtt[g % 2][:]),
                inc=16, serial=False, after=[dt_v[g]])

    p.barrier_wait("gpsimd", [dt_st[-1], t_bc] + st_xn_toks +
                   [st_xc_tok[m] for m in range(32)])
    p.finalize()
    return nc, es



DSH = DIN // NCORES
TC = 1024
NCH_SC = L // TC


def build_scan():
    nc = bass.Bass("TRN2", target_bir_lowering=False, debug=False)
    es = contextlib.ExitStack()
    p = Prog(nc, es)

    dt_d = nc.dram_tensor("dts", [DSH, B * L], F32, kind="ExternalInput").ap()
    xc_d = nc.dram_tensor("xcs", [DSH, B * L], F32, kind="ExternalInput").ap()
    bc_d = nc.dram_tensor("BCs", [32, B * L], F32, kind="ExternalInput").ap()
    al_d = nc.dram_tensor("Als", [DSH, DST], F32, kind="ExternalInput").ap()
    sel_d = nc.dram_tensor("sel", [32, 32 * 128], F32, kind="ExternalInput").ap()
    y_d = nc.dram_tensor("ys", [DSH, B * L], F32, kind="ExternalOutput").ap()

    bcs = sb(nc, es, "bcs", [32, B * L])
    sel = sb(nc, es, "sel_t", [32, 32 * 128])
    alog = sb(nc, es, "alog", [128, 2 * DST])
    aneg = sb(nc, es, "aneg", [128, 2 * DST])
    dtc = [sb(nc, es, f"dtc{j}", [128, 2 * TC]) for j in range(2)]
    xcc = [sb(nc, es, f"xcc{j}", [128, 2 * TC]) for j in range(2)]
    dtx = sb(nc, es, "dtx", [128, 2 * TC])
    abuf = [sb(nc, es, f"abuf{j}", [128, 2 * TC]) for j in range(2)]
    ubuf = sb(nc, es, "ubuf", [128, 2 * TC])
    hbuf = sb(nc, es, "hbuf", [128, 2 * TC])
    ptile = sb(nc, es, "ptile", [128, 2 * TC])
    yacc = [sb(nc, es, f"yacc{j}", [128, 2 * TC]) for j in range(2)]
    state = sb(nc, es, "state", [128, 2 * DST])

    psBC = [ps(nc, es, f"psBC{j}", [128, 2 * TC]) for j in range(2)]

    DMA = lambda out, in_: (lambda s: s.dma_start(out, in_))
    t_bc = p.op("sync", DMA(bcs[:], bc_d[:]), inc=16, serial=False)
    t_sel = p.op("sync", DMA(sel[:], sel_d[:]), inc=16, serial=False)
    t_al = p.op("sync", DMA(alog[:].rearrange("p (i s) -> p i s", s=DST),
                            al_d.rearrange("(i p) s -> p i s", p=128)),
                inc=16, serial=False)
    t_ae = p.op("scalar", lambda s: s.activation(aneg[:], alog[:], AF.Exp),
                after=[t_al])
    t_an = p.op("vector", lambda v: v.tensor_scalar(
        aneg[:], aneg[:], -1.0, None, ALU.mult), after=[t_ae])
    aneg3 = aneg[:].rearrange("p (i s) -> p i s", s=DST)

    stz_tok = p.op("vector", lambda v: v.memset(state[:], 0.0), serial=False)

    nchunks = B * NCH_SC
    dtx_toks = [None] * nchunks
    aexp_last = [None] * (nchunks * DST)
    u_toks = [None] * (nchunks * DST)
    p_toks = [None] * (nchunks * DST)
    sc_toks = [None] * (nchunks * DST)
    st_toks = [None] * nchunks

    for ci in range(nchunks):
        b_, jc = divmod(ci, NCH_SC)
        jj = ci % 2
        c0 = b_ * L + jc * TC
        a = []
        if ci >= 2:
            a.append(dtx_toks[ci - 2])
            a.append(aexp_last[(ci - 1) * DST - 1])
        p.op("sync", lambda s, c0=c0, jj=jj: s.dma_start(
            dtc[jj][:].rearrange("p (i t) -> p i t", t=TC),
            dt_d[:, c0:c0 + TC].rearrange("(i p) t -> p i t", p=128)),
            inc=16, serial=False, after=a)
        tl2 = p.op("sync", lambda s, c0=c0, jj=jj: s.dma_start(
            xcc[jj][:].rearrange("p (i t) -> p i t", t=TC),
            xc_d[:, c0:c0 + TC].rearrange("(i p) t -> p i t", p=128)),
            inc=16, serial=False)
        a = [tl2]
        if ci >= 1:
            a.append(u_toks[ci * DST - 1])
        dtx_toks[ci] = p.op("vector", lambda v, jj=jj: v.tensor_tensor(
            dtx[:], dtc[jj][:], xcc[jj][:], ALU.mult), after=a)
        for s_ in range(DST):
            g = ci * DST + s_
            jp = g % 2
            a_pe = [t_sel, t_bc]
            if g >= 2:
                a_pe.append(p_toks[g - 2])
            tk = None
            for half in range(2):
                for which in range(2):
                    row = s_ if which == 0 else 16 + s_
                    off = which * TC
                    tk = p.op("tensor", lambda t, row=row, off=off, half=half,
                              jp=jp, c0=c0: t.matmul(
                                  psBC[jp][:, off + half * 512:
                                           off + (half + 1) * 512],
                                  sel[:, 128 * row:128 * (row + 1)],
                                  bcs[0:32, c0 + half * 512:
                                      c0 + (half + 1) * 512],
                                  start=True, stop=True),
                              serial=False,
                              after=a_pe if (half == 0 and which == 0) else ())
            a_s = [dtx_toks[ci], t_an]
            if g >= 2:
                a_s.append(sc_toks[g - 2])
            tse = None
            for i in range(2):
                tse = p.op("scalar", lambda sc, i=i, s_=s_, jp=jp, jj=jj:
                           sc.activation(abuf[jp][:, i * TC:(i + 1) * TC],
                                         dtc[jj][:, i * TC:(i + 1) * TC],
                                         AF.Exp, scale=aneg3[:, i, s_:s_ + 1]),
                           after=a_s if i == 0 else ())
            aexp_last[g] = tse
            ty = None
            for i in range(2):
                tu = p.op("vector", lambda v, i=i, jp=jp: v.tensor_tensor(
                    ubuf[:, i * TC:(i + 1) * TC], dtx[:, i * TC:(i + 1) * TC],
                    psBC[jp][:, 0:TC], ALU.mult),
                    after=[dtx_toks[ci], tk, tse, stz_tok])
                tsc = p.op("vector", lambda v, i=i, jp=jp, s_=s_:
                           v.tensor_tensor_scan(
                               hbuf[:, i * TC:(i + 1) * TC],
                               abuf[jp][:, i * TC:(i + 1) * TC],
                               ubuf[:, i * TC:(i + 1) * TC],
                               state[:, i * DST + s_:i * DST + s_ + 1],
                               ALU.mult, ALU.add))
                tp_ = p.op("vector", lambda v, i=i, jp=jp: v.tensor_tensor(
                    ptile[:, i * TC:(i + 1) * TC],
                    hbuf[:, i * TC:(i + 1) * TC],
                    psBC[jp][:, TC:2 * TC], ALU.mult))
                p.op("vector", lambda v, i=i, s_=s_: v.tensor_copy(
                    state[:, i * DST + s_:i * DST + s_ + 1],
                    hbuf[:, i * TC + TC - 1:i * TC + TC]))
                a_y = []
                if ci >= 2 and s_ == 0 and i == 0:
                    a_y.append(st_toks[ci - 2])
                if s_ == 0:
                    ty = p.op("vector", lambda v, i=i, ci=ci: v.tensor_copy(
                        yacc[ci % 2][:, i * TC:(i + 1) * TC],
                        ptile[:, i * TC:(i + 1) * TC]), after=a_y)
                else:
                    ty = p.op("vector", lambda v, i=i, ci=ci: v.tensor_tensor(
                        yacc[ci % 2][:, i * TC:(i + 1) * TC],
                        yacc[ci % 2][:, i * TC:(i + 1) * TC],
                        ptile[:, i * TC:(i + 1) * TC], ALU.add), after=a_y)
                u_toks[g] = tu
                p_toks[g] = tp_
                sc_toks[g] = tsc
        st_toks[ci] = p.op("gdma", lambda g_, c0=c0, ci=ci: g_.dma_start(
            y_d[:, c0:c0 + TC].rearrange("(i p) t -> p i t", p=128),
            yacc[ci % 2][:].rearrange("p (i t) -> p i t", t=TC)), inc=16,
            serial=False, after=[ty])
        if b_ == 0 and jc == NCH_SC - 1:
            stz_tok = p.op("vector", lambda v: v.memset(state[:], 0.0))

    p.barrier_wait("gpsimd", st_toks)
    p.finalize()
    return nc, es



FP4_SHARE = DI4 // NCORES
WD_SHARE = D // NCORES
QW_SHARE = 6 * D // NCORES
WIN_SHARE = 3 * D // NCORES
FP4_MIDW = [0.25, 0.75, 1.25, 1.75, 2.5, 3.5, 5.0]


def build_prep():
    nc = bass.Bass("TRN2", target_bir_lowering=False, debug=False)
    es = contextlib.ExitStack()
    p = Prog(nc, es)

    wg_d = nc.dram_tensor("wg", [FP4_SHARE, D], F32, kind="ExternalInput").ap()
    wu_d = nc.dram_tensor("wu", [FP4_SHARE, D], F32, kind="ExternalInput").ap()
    wd_d = nc.dram_tensor("wd", [WD_SHARE, DI4], F32, kind="ExternalInput").ap()
    qw_d = nc.dram_tensor("qw", [QW_SHARE, D], I32, kind="ExternalInput").ap()
    sc_d = nc.dram_tensor("qsc", [QW_SHARE, 64], F32, kind="ExternalInput").ap()
    wi_d = nc.dram_tensor("win", [WIN_SHARE, D], F32, kind="ExternalInput").ap()
    wo_d = nc.dram_tensor("wout", [D // NCORES, D], F32, kind="ExternalInput").ap()
    la_d = nc.dram_tensor("la", [32, D], F32, kind="ExternalInput").ap()
    lb_d = nc.dram_tensor("lb", [D, 32], F32, kind="ExternalInput").ap()
    idb_d = nc.dram_tensor("idenb", [128, 128], BF16, kind="ExternalInput").ap()

    wgT_d = nc.dram_tensor("wgT", [D, FP4_SHARE], BF16, kind="ExternalOutput").ap()
    wuT_d = nc.dram_tensor("wuT", [D, FP4_SHARE], BF16, kind="ExternalOutput").ap()
    wdT_d = nc.dram_tensor("wdT", [DI4, WD_SHARE], BF16, kind="ExternalOutput").ap()
    qwT_d = nc.dram_tensor("qwT", [D, QW_SHARE], BF16, kind="ExternalOutput").ap()
    wiT_d = nc.dram_tensor("winT", [D, WIN_SHARE], BF16, kind="ExternalOutput").ap()
    woT_d = nc.dram_tensor("woutT", [D, D // NCORES], BF16, kind="ExternalOutput").ap()
    laT_d = nc.dram_tensor("laT", [D, 32], BF16, kind="ExternalOutput").ap()
    lbT_d = nc.dram_tensor("lbT", [32, D], BF16, kind="ExternalOutput").ap()

    W4 = DI4
    srcf = [sb(nc, es, f"srcf{j}", [128, W4]) for j in range(2)]
    srci = sb(nc, es, "srci", [128, D], I32)
    scrow = sb(nc, es, "scrow", [128, 64])
    a6 = sb(nc, es, "a6", [128, W4])
    sgn = sb(nc, es, "sgn", [128, W4])
    bmt = sb(nc, es, "bmt", [128, 176 * 9])
    cbuf = [sb(nc, es, f"cbuf{j}", [128, W4]) for j in range(2)]
    accq = sb(nc, es, "accq", [128, W4])
    qout = sb(nc, es, "qout", [128, W4], BF16)
    idenb = sb(nc, es, "idenb_t", [128, 128], BF16)
    evb = sb(nc, es, "evb", [128, W4], BF16)
    lbbuf = sb(nc, es, "lbbuf", [128, 8 * 32], BF16)

    psQ = [ps(nc, es, f"psQ{j}", [128, 1024], BF16) for j in range(2)]

    DMA = lambda out, in_: (lambda s: s.dma_start(out, in_))
    t_idb = p.op("sync", DMA(idenb[:], idb_d[:]), inc=16, serial=False)

    tp_hist = []
    st_last = [None]

    def transpose_store(src_tok, nrows, nb, dst_dram, r0):
        for nbk in range(nb):
            j = len(tp_hist) % 2
            a = [src_tok, t_idb]
            if len(tp_hist) >= 2:
                a.append(tp_hist[-2][1])
            tt = p.op("tensor", lambda t, nbk=nbk, j=j, nrows=nrows:
                      t.transpose(psQ[j][:, 0:nrows],
                                  qout[0:nrows, 128 * nbk:128 * (nbk + 1)],
                                  idenb[0:nrows, 0:nrows]), serial=False,
                      after=a)
            a_ev = [tt]
            if st_last[0] is not None and nbk == 0:
                a_ev.append(st_last[0])
            te = p.op("scalar", lambda s, nbk=nbk, j=j, nrows=nrows: s.copy(
                evb[:, nrows * nbk:nrows * (nbk + 1)], psQ[j][:, 0:nrows]),
                after=a_ev)
            tp_hist.append((tt, te))
        tst = p.op("gdma", lambda g, nrows=nrows, nb=nb, r0=r0,
                   dst_dram=dst_dram: g.dma_start(
                       dst_dram[:, r0:r0 + nrows].rearrange(
                           "(nb q) r -> q nb r", q=128),
                       evb[:, 0:nrows * nb].rearrange(
                           "q (nb r) -> q nb r", r=nrows)), inc=16,
                   serial=False, after=[tp_hist[-1][1]])
        st_last[0] = tst
        return tst

    def emit_fp4(src_dram, nrows_list, W, dst_dram, prev_q):
        nblk = W // 16
        r0 = 0
        for ti, nrows in enumerate(nrows_list):
            j = ti % 2
            a = []
            if prev_q.get(j) is not None:
                a.append(prev_q[j])
            tl = p.op("sync", lambda s, j=j, r0=r0, nrows=nrows, W=W,
                      src_dram=src_dram: s.dma_start(
                          srcf[j][0:nrows, 0:W], src_dram[r0:r0 + nrows, :]),
                      inc=16, serial=False, after=a)
            t_a6 = p.op("scalar", lambda s, j=j, nrows=nrows, W=W:
                        s.activation(a6[0:nrows, 0:W], srcf[j][0:nrows, 0:W],
                                     AF.Abs, scale=6.0), after=[tl])
            t_sg = p.op("scalar", lambda s, j=j, nrows=nrows, W=W:
                        s.activation(sgn[0:nrows, 0:W], srcf[j][0:nrows, 0:W],
                                     AF.Sign))
            p.op("vector", lambda v, j=j, nrows=nrows, W=W, nblk=nblk:
                 v.tensor_reduce(
                     bmt[0:nrows, 0:nblk],
                     srcf[j][0:nrows, 0:W].rearrange("p (n k) -> p n k", k=16),
                     AX.X, ALU.max, apply_absolute_value=True), after=[tl])
            for i, mid in enumerate(FP4_MIDW):
                p.op("vector", lambda v, i=i, mid=mid, nrows=nrows, nblk=nblk:
                     v.tensor_scalar(bmt[0:nrows, (i + 1) * nblk:(i + 2) * nblk],
                                     bmt[0:nrows, 0:nblk], mid, mid * 6e-12,
                                     ALU.mult, ALU.max))
            p.op("vector", lambda v, nrows=nrows, nblk=nblk: v.tensor_scalar(
                bmt[0:nrows, 8 * nblk:9 * nblk], bmt[0:nrows, 0:nblk],
                1.0 / 12.0, 1e-12 / 12.0, ALU.mult, ALU.max))

            def cmp(v, i, dstbuf, nrows=nrows, W=W, nblk=nblk):
                return v.tensor_tensor(
                    dstbuf[0:nrows, 0:W], a6[0:nrows, 0:W],
                    _rep16(bmt, nrows, (i + 1) * nblk, nblk), ALU.is_gt)

            p.op("vector", lambda v, cmp=cmp: cmp(v, 0, accq), after=[t_a6])
            for i in range(1, 4):
                p.op("vector", lambda v, i=i, cmp=cmp: cmp(v, i, cbuf[0]))
                p.op("vector", lambda v, nrows=nrows, W=W: v.tensor_tensor(
                    accq[0:nrows, 0:W], accq[0:nrows, 0:W],
                    cbuf[0][0:nrows, 0:W], ALU.add))
            p.op("vector", lambda v, cmp=cmp: cmp(v, 4, cbuf[0]))
            p.op("vector", lambda v, cmp=cmp: cmp(v, 5, cbuf[1]))
            p.op("vector", lambda v, nrows=nrows, W=W: v.tensor_tensor(
                cbuf[0][0:nrows, 0:W], cbuf[0][0:nrows, 0:W],
                cbuf[1][0:nrows, 0:W], ALU.add))
            p.op("vector", lambda v, nrows=nrows, W=W: v.scalar_tensor_tensor(
                accq[0:nrows, 0:W], cbuf[0][0:nrows, 0:W], 2.0,
                accq[0:nrows, 0:W], ALU.mult, ALU.add))
            p.op("vector", lambda v, cmp=cmp: cmp(v, 6, cbuf[1]))
            p.op("vector", lambda v, nrows=nrows, W=W: v.scalar_tensor_tensor(
                accq[0:nrows, 0:W], cbuf[1][0:nrows, 0:W], 4.0,
                accq[0:nrows, 0:W], ALU.mult, ALU.add))
            p.op("vector", lambda v, nrows=nrows, W=W, nblk=nblk:
                 v.tensor_tensor(accq[0:nrows, 0:W], accq[0:nrows, 0:W],
                                 _rep16(bmt, nrows, 8 * nblk, nblk), ALU.mult))
            a_q = [t_sg]
            if tp_hist:
                a_q.append(tp_hist[-1][0])
            tq = p.op("vector", lambda v, nrows=nrows, W=W: v.tensor_tensor(
                qout[0:nrows, 0:W], accq[0:nrows, 0:W], sgn[0:nrows, 0:W],
                ALU.mult), after=a_q)
            prev_q[j] = tq
            transpose_store(tq, nrows, W // 128, dst_dram, r0)
            r0 += nrows
        return prev_q

    def emit_dequant(r0, prev_tok):
        tl1 = p.op("sync", DMA(srci[:], qw_d[r0:r0 + 128, :]), inc=16,
                   serial=False, after=[prev_tok] if prev_tok else [])
        tl2 = p.op("sync", DMA(scrow[:], sc_d[r0:r0 + 128, :]), inc=16,
                   serial=False)
        t_cv = p.op("scalar", lambda s: s.activation(
            a6[:, 0:D], srci[:], AF.Identity, scale=1.0 / 7.5), after=[tl1])
        a_q = [t_cv, tl2]
        if tp_hist:
            a_q.append(tp_hist[-1][0])
        tq = p.op("vector", lambda v: v.scalar_tensor_tensor(
            qout[:, 0:D], a6[:, 0:D], -1.0, _rep16(scrow, 128, 0, 64),
            ALU.add, ALU.mult), after=a_q)
        transpose_store(tq, 128, D // 128, qwT_d, r0)
        return tq

    def emit_plain(src_dram, nrows_list, W, dst_dram, prev_tok):
        r0 = 0
        tq = prev_tok
        for ti, nrows in enumerate(nrows_list):
            tl = p.op("sync", lambda s, r0=r0, nrows=nrows, W=W,
                      src_dram=src_dram: s.dma_start(
                          srcf[0][0:nrows, 0:W], src_dram[r0:r0 + nrows, :]),
                      inc=16, serial=False, after=[tq] if tq else [])
            a_q = [tl]
            if tp_hist:
                a_q.append(tp_hist[-1][0])
            tq = p.op("vector", lambda v, nrows=nrows, W=W: v.tensor_copy(
                qout[0:nrows, 0:W], srcf[0][0:nrows, 0:W]), after=a_q)
            transpose_store(tq, nrows, W // 128, dst_dram, r0)
            r0 += nrows
        return tq

    pq = {}
    pq = emit_fp4(wg_d, [128, 128, 96], D, wgT_d, pq)
    pq = emit_fp4(wu_d, [128, 128, 96], D, wuT_d, pq)
    pq = emit_fp4(wd_d, [128], DI4, wdT_d, pq)
    tq = pq.get(0)
    for ti in range(QW_SHARE // 128):
        tq = emit_dequant(128 * ti, tq)
    tq = emit_plain(wi_d, [128, 128, 128], D, wiT_d, tq)
    tq = emit_plain(wo_d, [128], D, woT_d, tq)
    tla = p.op("sync", DMA(srcf[0][0:32, 0:D], la_d[:]), inc=16, serial=False,
               after=[tq])
    tlb = p.op("sync", lambda s: s.dma_start(
        srcf[1][:, 0:8 * 32].rearrange("p (i r) -> p i r", r=32),
        lb_d.rearrange("(i p) r -> p i r", p=128)), inc=16, serial=False)
    tq = p.op("vector", lambda v: v.tensor_copy(qout[0:32, 0:D],
                                                srcf[0][0:32, 0:D]),
              after=[tla, tp_hist[-1][0]])
    t_lb = p.op("vector", lambda v: v.tensor_copy(lbbuf[:],
                                                  srcf[1][:, 0:8 * 32]),
                after=[tlb])
    transpose_store(tq, 32, D // 128, laT_d, 0)
    for i in range(8):
        j = len(tp_hist) % 2
        a = [t_lb]
        if len(tp_hist) >= 2:
            a.append(tp_hist[-2][1])
        tt = p.op("tensor", lambda t, i=i, j=j: t.transpose(
            psQ[j][0:32, 0:128],
            lbbuf[:].rearrange("p (i r) -> p i r", r=32)[:, i, :],
            idenb[:]), serial=False, after=a)
        a_ev = [tt]
        if i == 0:
            a_ev.append(st_last[0])
        te = p.op("scalar", lambda s, i=i, j=j: s.copy(
            evb[0:32, 128 * i:128 * (i + 1)], psQ[j][0:32, 0:128]),
            after=a_ev)
        tp_hist.append((tt, te))
    tst = p.op("gdma", DMA(lbT_d[:], evb[0:32, 0:D]), inc=16, serial=False,
               after=[tp_hist[-1][1]])
    p.barrier_wait("gpsimd", [tst])
    p.finalize()
    return nc, es



def build_l0s():
    nc = bass.Bass("TRN2", target_bir_lowering=False, debug=False)
    es = contextlib.ExitStack()
    p = Prog(nc, es)

    yT_d = nc.dram_tensor("yT", [DIN, T], F32, kind="ExternalInput").ap()
    xcT_d = nc.dram_tensor("xcT", [DIN, T], F32, kind="ExternalInput").ap()
    zT_d = nc.dram_tensor("zT", [DIN, T], F32, kind="ExternalInput").ap()
    xn_d = nc.dram_tensor("xn", [T, D], F32, kind="ExternalInput").ap()
    dv_d = nc.dram_tensor("Dv", [1, DIN], F32, kind="ExternalInput").ap()
    opw_d = nc.dram_tensor("opw", [DIN, D], F32, kind="ExternalInput").ap()
    g2_d = nc.dram_tensor("ln2_g", [1, D], F32, kind="ExternalInput").ap()
    b2_d = nc.dram_tensor("ln2_b", [1, D], F32, kind="ExternalInput").ap()
    w1_d = nc.dram_tensor("ffn_w1", [D, DFF], F32, kind="ExternalInput").ap()
    fb1_d = nc.dram_tensor("ffn_b1", [1, DFF], F32, kind="ExternalInput").ap()
    w2_d = nc.dram_tensor("ffn_w2", [DFF, D], F32, kind="ExternalInput").ap()
    fb2_d = nc.dram_tensor("ffn_b2", [1, D], F32, kind="ExternalInput").ap()
    idn_d = nc.dram_tensor("iden", [128, 128], F32, kind="ExternalInput").ap()

    s_d = nc.dram_tensor("s_out", [T, D], F32, kind="ExternalOutput").ap()
    y3_i = nc.dram_tensor("y3T_i", [DIN, T], BF16, kind="Internal").ap()

    xn = sb(nc, es, "xn_t", [128, NT * D])
    stat = sb(nc, es, "statd", [128, 9 * 16])
    scr = sb(nc, es, "scrd", [128, D])
    ytl = [sb(nc, es, "ytl0", [128, T])] * 2
    xct = sb(nc, es, "xct", [128, T])
    ztl = sb(nc, es, "ztl", [128, T])
    t1b = sb(nc, es, "t1b", [128, T])
    szb = sb(nc, es, "szb", [128, T])
    y3b = [sb(nc, es, f"y3b{j}", [128, T], BF16) for j in range(2)]
    wopb = sb(nc, es, "wopb", [128, 16 * D], BF16)
    y3sl = [sb(nc, es, f"y3sl{j}", [128, 128], BF16) for j in range(2)]
    wsl = [sb(nc, es, "wsl0", [128, 1024])] * 2
    wslb = [sb(nc, es, "wslb0", [128, 1024], BF16)] * 2
    gffb = sb(nc, es, "gffb", [128, 32 * 256], BF16)
    ln2pp = [sb(nc, es, f"ln2pp{j}", [128, D]) for j in range(2)]
    ln2b16 = sb(nc, es, "ln2b16", [128, D], BF16)
    iden = sb(nc, es, "iden_t", [128, 128])
    idenb = sb(nc, es, "idenb2", [128, 128], BF16)
    one1 = sb(nc, es, "one1d", [1, 128])
    rowbuf = sb(nc, es, "rowbuf", [1, DIN])
    dvcols = sb(nc, es, "dvcols", [128, 16])
    b1cols = sb(nc, es, "b1cols", [128, 32])
    g2b = sb(nc, es, "g2b", [128, D])
    b2b = sb(nc, es, "b2b", [128, D])
    fb2b = sb(nc, es, "fb2b", [128, D])

    ps1 = ps(nc, es, "ps1", [128, 512])
    ps2 = ps(nc, es, "ps2", [128, 512])
    psS = [ps(nc, es, f"psS{j}", [128, 512]) for j in range(4)]
    psq = [ps(nc, es, f"psq{j}", [128, 1024], BF16) for j in range(2)]

    xn3 = xn[:].rearrange("p (i d) -> p i d", d=D)
    wopb3 = wopb[:].rearrange("p (k n) -> p k n", n=D)
    gffb3 = gffb[:].rearrange("p (k n) -> p k n", n=256)
    DMA = lambda out, in_: (lambda s: s.dma_start(out, in_))

    t_xn = p.op("sync", DMA(xn3, xn_d.rearrange("(i p) d -> p i d", p=128)))
    t_idn = p.op("sync", DMA(iden[:], idn_d[:]))
    t_g2 = p.op("sync", DMA(rowbuf[0:1, 0:D], g2_d[:]))
    t_one = p.op("vector", lambda v: v.memset(one1[:], 1.0), serial=False)
    t_stm = p.op("vector", lambda v: v.memset(stat[:], 1.0), serial=False)
    t_idb = p.op("vector", lambda v: v.tensor_copy(idenb[:], iden[:]),
                 after=[t_idn])

    def row_bcast(dst, row_tok, width, extra=()):
        ev = None
        for h in range(width // 512):
            a = [t_one, row_tok] + list(extra)
            if ev is not None:
                a.append(ev)
            tpe = p.op("tensor", lambda t, h=h: t.matmul(
                ps1[:, 0:512], one1[0:1, 0:128],
                rowbuf[0:1, 512 * h:512 * (h + 1)], start=True, stop=True),
                serial=False, after=a)
            ev = p.op("scalar", lambda s, h=h, dst=dst: s.copy(
                dst[:, 512 * h:512 * (h + 1)], ps1[:, 0:512]), after=[tpe])
        return ev

    def col_load(dst, row_tok, ncols, extra=()):
        tk = None
        for i in range(ncols):
            tk = p.op("tensor", lambda t, i=i: t.matmul(
                ps1[:, i:i + 1] if i < 512 else None,
                rowbuf[0:1, 128 * i:128 * (i + 1)],
                one1[0:1, 0:1], start=True, stop=True), serial=False,
                after=[t_one, row_tok] + list(extra) if i == 0 else ())
        return p.op("scalar", lambda s, dst=dst, ncols=ncols: s.copy(
            dst[:], ps1[:, 0:ncols]), after=[tk])

    t_g2b = row_bcast(g2b, t_g2, D)
    t_b2l = p.op("sync", DMA(rowbuf[0:1, 0:D], b2_d[:]), after=[t_g2b])
    t_b2b = row_bcast(b2b, t_b2l, D, extra=[t_g2b])
    t_fl = p.op("sync", DMA(rowbuf[0:1, 0:D], fb2_d[:]), after=[t_b2b])
    t_fb2b = row_bcast(fb2b, t_fl, D, extra=[t_b2b])
    t_dvl = p.op("sync", DMA(rowbuf[0:1, 0:DIN], dv_d[:]), after=[t_fb2b])
    t_dvc = col_load(dvcols, t_dvl, 16, extra=[t_fb2b])
    t_b1l = p.op("sync", DMA(rowbuf[0:1, 0:DIN], fb1_d[0:1, 0:DIN]),
                 after=[t_dvc])
    t_b1c_a = col_load(b1cols[:, 0:16], t_b1l, 16, extra=[t_dvc])
    t_b1l2 = p.op("sync", DMA(rowbuf[0:1, 0:DIN], fb1_d[0:1, DIN:DFF]),
                  after=[t_b1c_a])
    t_b1c = col_load(b1cols[:, 16:32], t_b1l2, 16, extra=[t_b1c_a])

    opw_toks = []
    for k in range(16):
        j = k % 2
        a = [t_b1c] if k == 0 else []
        if k >= 1:
            a.append(opw_toks[k - 1])
        tlw = p.op("sync", DMA(wsl[j][:, 0:D], opw_d[128 * k:128 * (k + 1), :]),
                   after=a)
        opw_toks.append(p.op("vector", lambda v, j=j, k=k: v.tensor_copy(
            wopb3[:, k, :], wsl[j][:, 0:D]), after=[tlw]))
    t_opw = opw_toks[-1]

    y3_st = []
    y3_toks = []
    for m in range(16):
        j = m % 2
        a = []
        if m >= 1:
            a.append(y3_toks[m - 1])
        tly = p.op("sync", DMA(ytl[j][:], yT_d[128 * m:128 * (m + 1), :]),
                   after=a)
        tlx = p.op("sync", DMA(xct[:], xcT_d[128 * m:128 * (m + 1), :]))
        tlz = p.op("sync", DMA(ztl[:], zT_d[128 * m:128 * (m + 1), :]))
        tt1 = p.op("vector", lambda v, j=j, m=m: v.scalar_tensor_tensor(
            t1b[:], xct[:], dvcols[:, m:m + 1], ytl[j][:], ALU.mult, ALU.add),
            after=[tly, tlx, t_dvc])
        tsz = p.op("scalar", lambda s: s.activation(szb[:], ztl[:], AF.Silu),
                   after=[tlz])
        a3 = [tsz]
        if m >= 2:
            a3.append(y3_st[m - 2])
        ty3 = p.op("vector", lambda v, j=j: v.tensor_tensor(
            y3b[j][:], t1b[:], szb[:], ALU.mult), after=a3)
        y3_toks.append(ty3)
        y3_st.append(p.op("gdma", lambda g_, m=m, j=j: g_.dma_start(
            y3_i[128 * m:128 * (m + 1), :], y3b[j][:]), after=[ty3]))

    xmid_toks = []
    mm_out = []
    for i in range(NT):
        tk1 = tk2 = None
        for k in range(16):
            j = (i * 16 + k) % 2
            a = [y3_st[k]]
            if len(mm_out) >= 2:
                a.append(mm_out[-2])
            tsl = p.op("sync", lambda s, i=i, k=k, j=j: s.dma_start(
                y3sl[j][:], y3_i[128 * k:128 * (k + 1), 128 * i:128 * (i + 1)]),
                after=a)
            a2 = [tsl, t_opw]
            if i >= 1 and k == 0:
                a2.append(xmid_toks[-1])
            tk1 = p.op("tensor", lambda t, i=i, k=k, j=j: t.matmul(
                ps1[:, 0:512], y3sl[j][:], wopb3[:, k, 0:512],
                start=(k == 0), stop=(k == 15)), serial=False, after=a2)
            tk2 = p.op("tensor", lambda t, i=i, k=k, j=j: t.matmul(
                ps2[:, 0:512], y3sl[j][:], wopb3[:, k, 512:1024],
                start=(k == 0), stop=(k == 15)), serial=False)
            mm_out.append(tk2)
        p.op("vector", lambda v, i=i: v.tensor_tensor(
            xn3[:, i, 0:512], xn3[:, i, 0:512], ps1[:, 0:512], ALU.add),
            after=[tk1, t_xn])
        xmid_toks.append(p.op("vector", lambda v, i=i: v.tensor_tensor(
            xn3[:, i, 512:1024], xn3[:, i, 512:1024], ps2[:, 0:512], ALU.add),
            after=[tk2]))

    hT3 = wopb[:].rearrange("p (k t) -> p k t", t=T)
    tilesC = [(xn3[:, i, :], 128, xmid_toks[i]) for i in range(NT)]
    p.barrier_wait("scalar", [t_stm])
    M2, R2, tR2 = emit_ln_stats(p, tilesC, stat, scr)
    tp_toks = []
    ev_toks = []
    for i in range(NT):
        buf = ln2pp[i % 2]
        a = [tR2, t_g2b, t_b2b, t_opw]
        if i >= 2:
            a.append(tp_toks[8 * (i - 1) - 1])
        p.op("vector", lambda v, i=i, buf=buf: v.tensor_scalar(
            buf[:], xn3[:, i, :], M2[:, i:i + 1], R2[:, i:i + 1],
            ALU.subtract, ALU.mult), after=a)
        p.op("vector", lambda v, buf=buf: v.tensor_tensor(
            buf[:], buf[:], g2b[:], ALU.mult))
        a_t3 = []
        if i >= 1:
            a_t3.append(tp_toks[-1])
        t3 = p.op("vector", lambda v, buf=buf: v.tensor_tensor(
            ln2b16[:], buf[:], b2b[:], ALU.add), after=a_t3)
        for k in range(8):
            n_tp = len(tp_toks)
            j = n_tp % 2
            a = [t3, t_idb]
            if n_tp >= 2:
                a.append(ev_toks[n_tp - 2])
            tt = p.op("tensor", lambda t, k=k, j=j: t.transpose(
                psq[j][:, 0:128], ln2b16[:, 128 * k:128 * (k + 1)], idenb[:]),
                serial=False, after=a)
            te = p.op("scalar", lambda s, k=k, j=j, i=i: s.copy(
                hT3[:, k, 128 * i:128 * (i + 1)], psq[j][:, 0:128]),
                after=[tt])
            tp_toks.append(tt)
            ev_toks.append(te)
    t_hT = ev_toks[-1]

    w1_toks = {}
    w1mm = {}
    w2_toks = {}
    w2mm = {}
    gf_toks = []
    s_st = []
    w2_last = None
    NCHF = T // 256
    for tc in range(NCHF):
        c0 = 256 * tc
        for mf in range(32):
            gidx = tc * 32 + mf
            j = gidx % 2
            pp = ps1 if gidx % 2 == 0 else ps2
            a = []
            if gidx >= 1:
                a.append(w1_toks[gidx - 1])
            else:
                a.append(t_opw)
            if mf == 0 and tc >= 1:
                a.append(w2_toks[tc * 32 - 1])
            tlw = p.op("sync", lambda s, mf=mf, j=j: s.dma_start(
                wsl[j][:, 0:1024].rearrange("p (k m) -> p k m", m=128),
                w1_d[:, 128 * mf:128 * (mf + 1)].rearrange(
                    "(k p) m -> p k m", p=128)), after=a)
            a_cw = [tlw]
            if gidx >= 1:
                a_cw.append(w1mm.get(gidx - 1))
            if tc >= 1 and mf == 0:
                a_cw.append(w2_last)
            tcw = p.op("vector", lambda v, j=j: v.tensor_copy(
                wslb[j][:, 0:1024], wsl[j][:, 0:1024]), after=a_cw)
            w1_toks[gidx] = tcw
            a2 = [tcw, t_hT]
            if len(gf_toks) >= 2:
                a2.append(gf_toks[-2])
            tk = None
            for k in range(8):
                tk = p.op("tensor", lambda t, k=k, c0=c0, j=j, pp=pp: t.matmul(
                    pp[:, 0:256],
                    wslb[j][:, 0:1024].rearrange("p (k m) -> p k m", m=128)[:, k, :],
                    hT3[:, k, c0:c0 + 256], start=(k == 0), stop=(k == 7)),
                    serial=False, after=a2 if k == 0 else ())
            w1mm[gidx] = tk
            a_gf = [tk, t_b1c]
            if tc >= 1 and mf == 0:
                a_gf.append(w2_last)
            gf_toks.append(p.op("scalar", lambda s, mf=mf, pp=pp: s.activation(
                gffb3[:, mf, :], pp[:, 0:256], AF.Gelu_apprx_tanh,
                bias=b1cols[:, mf:mf + 1]), after=a_gf))
        t_gff = gf_toks[-1]
        ev_w2 = []
        for k2 in range(32):
            gidx = tc * 32 + k2
            j = gidx % 2
            a = [w1_toks[tc * 32 + 31]]
            if gidx >= 1:
                a.append(w2_toks[gidx - 1])
            tlw = p.op("sync", lambda s, k2=k2, j=j: s.dma_start(
                wsl[j][:, 0:D], w2_d[128 * k2:128 * (k2 + 1), :]), after=a)
            a_cw = [tlw, w1mm[tc * 32 + 31]]
            if gidx >= 1:
                a_cw.append(w2mm.get(gidx - 1))
            tcw = p.op("vector", lambda v, j=j: v.tensor_copy(
                wslb[j][:, 0:D], wsl[j][:, 0:D]), after=a_cw)
            w2_toks[gidx] = tcw
            for sub in range(2):
                for half in range(2):
                    a2 = [tcw, t_gff]
                    if tc >= 1 and k2 == 0:
                        a2.append(s_st[-1])
                    w2mm[gidx] = p.op("tensor", lambda t, k2=k2, sub=sub,
                                      half=half, j=j, c0=c0: t.matmul(
                                          psS[2 * sub + half][:, 0:512],
                                          gffb3[:, k2, 128 * sub:128 * (sub + 1)],
                                          wslb[j][:, 512 * half:512 * (half + 1)],
                                          start=(k2 == 0), stop=(k2 == 31)),
                                      serial=False,
                                      after=a2 if (sub == 0 and half == 0)
                                      else ())
        mm_w2_last = p.tok("tensor")
        w2_last = mm_w2_last
        for sub in range(2):
            i_t = (c0 + 128 * sub) // 128
            tv1 = p.op("vector", lambda v, i_t=i_t, sub=sub: v.tensor_tensor(
                xn3[:, i_t, 0:512], xn3[:, i_t, 0:512], psS[2 * sub][:, 0:512],
                ALU.add), after=[mm_w2_last, t_fb2b])
            tv2 = p.op("vector", lambda v, i_t=i_t, sub=sub: v.tensor_tensor(
                xn3[:, i_t, 512:1024], xn3[:, i_t, 512:1024],
                psS[2 * sub + 1][:, 0:512], ALU.add))
            tv3 = p.op("vector", lambda v, i_t=i_t: v.tensor_tensor(
                xn3[:, i_t, :], xn3[:, i_t, :], fb2b[:], ALU.add))
            s_st.append(p.op("gdma", lambda g_, i_t=i_t: g_.dma_start(
                s_d[128 * i_t:128 * (i_t + 1), :], xn3[:, i_t, :]),
                after=[tv3]))

    p.barrier_wait("gpsimd", s_st)
    p.finalize()
    return nc, es



def build_l0g():
    nc = bass.Bass("TRN2", target_bir_lowering=False, debug=False)
    es = contextlib.ExitStack()
    p = Prog(nc, es)

    xn_d = nc.dram_tensor("xn", [T, D], F32, kind="ExternalInput").ap()
    s_d = nc.dram_tensor("s_in", [T, D], F32, kind="ExternalInput").ap()
    x_d = nc.dram_tensor("x", [T, D], F32, kind="ExternalInput").ap()
    wgT_d = nc.dram_tensor("wgT", [D, DI4], BF16, kind="ExternalInput").ap()
    wuT_d = nc.dram_tensor("wuT", [D, DI4], BF16, kind="ExternalInput").ap()
    wdT_d = nc.dram_tensor("wdT", [DI4, D], BF16, kind="ExternalInput").ap()
    idb_d = nc.dram_tensor("idenb", [128, 128], BF16, kind="ExternalInput").ap()

    x1_d = nc.dram_tensor("x1", [T, D], F32, kind="ExternalOutput").ap()

    wgt = sb(nc, es, "wgt", [128, 8 * DI4], BF16)
    wut = sb(nc, es, "wut", [128, 8 * DI4], BF16)
    wdt = sb(nc, es, "wdt", [128, 22 * D], BF16)
    a6q = sb(nc, es, "a6q", [128, DI4])
    sgq = sb(nc, es, "sgq", [128, DI4])
    bmq = sb(nc, es, "bmq", [128, 176 * 9])
    cq = [sb(nc, es, "cq0", [128, DI4])] * 2
    acq = sb(nc, es, "acq", [128, DI4])
    qtile = sb(nc, es, "qtile", [128, DI4], BF16)
    xqT = sb(nc, es, "xqT", [128, 8 * 128], BF16)
    hqT = sb(nc, es, "hqT", [128, 22 * 128], BF16)
    h4 = acq
    sg4 = sb(nc, es, "sg4", [128, D])
    sgate = sg4[:, 0:512]
    gt = sg4
    stl = sb(nc, es, "stl", [128, D])
    xnt = [stl] * 2
    xtl = sb(nc, es, "xtl", [128, D])
    x1t = stl
    idenb = sb(nc, es, "idenb_t", [128, 128], BF16)

    ps1 = ps(nc, es, "ps1", [128, 512])
    ps2 = ps(nc, es, "ps2", [128, 512])
    psD = [ps(nc, es, f"psD{j}", [128, 512]) for j in range(2)]
    psq = [ps(nc, es, f"psq{j}", [128, 1024], BF16) for j in range(2)]

    wgt3 = wgt[:].rearrange("p (k n) -> p k n", n=DI4)
    wut3 = wut[:].rearrange("p (k n) -> p k n", n=DI4)
    wdt3 = wdt[:].rearrange("p (k n) -> p k n", n=D)
    xqT3 = xqT[:].rearrange("p (k m) -> p k m", m=128)
    hqT3 = hqT[:].rearrange("p (k m) -> p k m", m=128)
    DMA = lambda out, in_: (lambda s: s.dma_start(out, in_))

    t_idb = p.op("sync", DMA(idenb[:], idb_d[:]))
    t_wg = p.op("sync", DMA(wgt3, wgT_d.rearrange("(k p) n -> p k n", p=128)))
    t_wu = p.op("sync", DMA(wut3, wuT_d.rearrange("(k p) n -> p k n", p=128)))
    t_wd = p.op("sync", DMA(wdt3, wdT_d.rearrange("(k p) n -> p k n", p=128)))

    def emit_quant(src_ap, nrows, W, out_ap, after):
        nblk = W // 16
        t_a6 = p.op("scalar", lambda s: s.activation(
            a6q[0:nrows, 0:W], src_ap, AF.Abs, scale=6.0), after=after)
        t_sg = p.op("scalar", lambda s: s.activation(
            sgq[0:nrows, 0:W], src_ap, AF.Sign))
        p.op("vector", lambda v: v.tensor_reduce(
            bmq[0:nrows, 0:nblk],
            src_ap.rearrange("p (n k) -> p n k", k=16),
            AX.X, ALU.max, apply_absolute_value=True), after=after)
        for i, mid in enumerate(FP4_MIDW):
            p.op("vector", lambda v, i=i, mid=mid: v.tensor_scalar(
                bmq[0:nrows, (i + 1) * nblk:(i + 2) * nblk],
                bmq[0:nrows, 0:nblk], mid, mid * 6e-12, ALU.mult, ALU.max))
        p.op("vector", lambda v: v.tensor_scalar(
            bmq[0:nrows, 8 * nblk:9 * nblk], bmq[0:nrows, 0:nblk],
            1.0 / 12.0, 1e-12 / 12.0, ALU.mult, ALU.max))

        def cmp(v, i, dstbuf):
            return v.tensor_tensor(
                dstbuf[0:nrows, 0:W], a6q[0:nrows, 0:W],
                _rep16(bmq, nrows, (i + 1) * nblk, nblk), ALU.is_gt)

        p.op("vector", lambda v, cmp=cmp: cmp(v, 0, acq), after=[t_a6, t_sg])
        for i in range(1, 4):
            p.op("vector", lambda v, i=i, cmp=cmp: cmp(v, i, cq[0]))
            p.op("vector", lambda v: v.tensor_tensor(
                acq[0:nrows, 0:W], acq[0:nrows, 0:W], cq[0][0:nrows, 0:W],
                ALU.add))
        for i, w in ((4, 2.0), (5, 2.0), (6, 4.0)):
            p.op("vector", lambda v, i=i, cmp=cmp: cmp(v, i, cq[0]))
            p.op("vector", lambda v, w=w: v.scalar_tensor_tensor(
                acq[0:nrows, 0:W], cq[0][0:nrows, 0:W], w, acq[0:nrows, 0:W],
                ALU.mult, ALU.add))
        p.op("vector", lambda v: v.tensor_tensor(
            acq[0:nrows, 0:W], acq[0:nrows, 0:W],
            _rep16(bmq, nrows, 8 * nblk, nblk), ALU.mult))
        return p.op("vector", lambda v: v.tensor_tensor(
            out_ap, acq[0:nrows, 0:W], sgq[0:nrows, 0:W], ALU.mult),
            after=[t_sg])

    tp_hist = []

    def transpose_to(src_ap, dst_ap, src_tok):
        j = len(tp_hist) % 2
        a = [src_tok, t_idb]
        if len(tp_hist) >= 2:
            a.append(tp_hist[-2][1])
        tt = p.op("tensor", lambda t, j=j, src_ap=src_ap: t.transpose(
            psq[j][:, 0:128], src_ap, idenb[:]), serial=False, after=a)
        te = p.op("scalar", lambda s, j=j, dst_ap=dst_ap: s.copy(
            dst_ap, psq[j][:, 0:128]), after=[tt])
        tp_hist.append((tt, te))
        return te

    NCH4 = [(0, 512), (512, 512), (1024, 512), (1536, 512), (2048, 512),
            (2560, 256)]
    st_x1 = []
    prev = {}
    for i in range(NT):
        jx = i % 2
        a = []
        if prev.get("xn_cons") is not None:
            a.append(prev["xn_cons"])
        t_lx = p.op("sync", DMA(xnt[jx][:], xn_d[128 * i:128 * (i + 1), :]),
                    after=a)
        a = [t_lx]
        if prev.get("xqT_cons") is not None:
            a.append(prev["xqT_cons"])
        if tp_hist:
            a.append(tp_hist[-1][0])
        tq = emit_quant(xnt[jx][:], 128, D, qtile[0:128, 0:D], a)
        prev["xn_cons"] = tq
        te = None
        for k in range(8):
            te = transpose_to(qtile[:, 128 * k:128 * (k + 1)],
                              xqT3[:, k, :], tq)
        h4_toks = []
        ev_prev = None
        for nci, (n0, nn) in enumerate(NCH4):
            a = [te, t_wg, t_wu]
            if ev_prev is not None:
                a.append(ev_prev)
            tk1 = tk2 = None
            for k in range(8):
                tk1 = p.op("tensor", lambda t, k=k, n0=n0, nn=nn: t.matmul(
                    ps1[:, 0:nn], xqT3[:, k, :], wgt3[:, k, n0:n0 + nn],
                    start=(k == 0), stop=(k == 7)), serial=False,
                    after=a if k == 0 else ())
                tk2 = p.op("tensor", lambda t, k=k, n0=n0, nn=nn: t.matmul(
                    ps2[:, 0:nn], xqT3[:, k, :], wut3[:, k, n0:n0 + nn],
                    start=(k == 0), stop=(k == 7)), serial=False)
            tsg = p.op("scalar", lambda s, nn=nn: s.activation(
                sgate[:, 0:nn], ps1[:, 0:nn], AF.Silu), after=[tk1])
            th4 = p.op("vector", lambda v, n0=n0, nn=nn: v.tensor_tensor(
                h4[:, n0:n0 + nn], sgate[:, 0:nn], ps2[:, 0:nn], ALU.mult),
                after=[tsg, tk2])
            ev_prev = th4
            h4_toks.append(th4)
        a = [h4_toks[-1]]
        if tp_hist:
            a.append(tp_hist[-1][0])
        tqh = emit_quant(h4[0:128, 0:DI4], 128, DI4, qtile[0:128, 0:DI4], a)
        teh = None
        for k in range(22):
            teh = transpose_to(qtile[:, 128 * k:128 * (k + 1)],
                               hqT3[:, k, :], tqh)
        prev["xqT_cons"] = tqh
        tdk = {}
        for half in range(2):
            a = [teh, t_wd]
            if i >= 1 and half == 0:
                a.append(st_x1[-1])
            tk = None
            for k2 in range(22):
                tk = p.op("tensor", lambda t, k2=k2, half=half: t.matmul(
                    psD[half][:, 0:512], hqT3[:, k2, :],
                    wdt3[:, k2, 512 * half:512 * (half + 1)],
                    start=(k2 == 0), stop=(k2 == 21)), serial=False,
                    after=a if k2 == 0 else ())
            tdk[half] = tk
        tg1 = p.op("scalar", lambda s: s.activation(
            gt[:, 0:512], psD[0][:, 0:512], AF.Silu), after=[tdk[0]])
        tg2 = p.op("scalar", lambda s: s.activation(
            gt[:, 512:1024], psD[1][:, 0:512], AF.Silu), after=[tdk[1]])
        t_ls = p.op("sync", DMA(stl[:], s_d[128 * i:128 * (i + 1), :]),
                    after=[tqh])
        t_lxx = p.op("sync", DMA(xtl[:], x_d[128 * i:128 * (i + 1), :]))
        tv1 = p.op("vector", lambda v: v.tensor_tensor(
            x1t[:], stl[:], gt[:], ALU.mult), after=[tg1, tg2, t_ls])
        tv2 = p.op("vector", lambda v: v.tensor_tensor(
            x1t[:], x1t[:], xtl[:], ALU.add), after=[t_lxx])
        tst = p.op("gdma", lambda g_, i=i: g_.dma_start(
            x1_d[128 * i:128 * (i + 1), :], x1t[:]), after=[tv2])
        st_x1.append(tst)
        prev["xn_cons"] = tst

    p.barrier_wait("gpsimd", st_x1)
    p.finalize()
    return nc, es



NHEAD = 16
HD = 64


def build_l1att():
    nc = bass.Bass("TRN2", target_bir_lowering=False, debug=False)
    es = contextlib.ExitStack()
    p = Prog(nc, es)

    x1_d = nc.dram_tensor("x1", [T, D], F32, kind="ExternalInput").ap()
    wiT_d = nc.dram_tensor("winT", [D, 3 * D], BF16, kind="ExternalInput").ap()
    bi_d = nc.dram_tensor("in_bias", [1, 3 * D], F32, kind="ExternalInput").ap()
    woT_d = nc.dram_tensor("woutT", [D, D], BF16, kind="ExternalInput").ap()
    bo_d = nc.dram_tensor("out_bias", [1, D], F32, kind="ExternalInput").ap()
    sg2_d = nc.dram_tensor("seg2", [128, 2], F32, kind="ExternalInput").ap()
    sg2b_d = nc.dram_tensor("seg2b", [2, 128], F32, kind="ExternalInput").ap()
    idn_d = nc.dram_tensor("iden", [128, 128], F32, kind="ExternalInput").ap()

    s2_d = nc.dram_tensor("s2", [T, D], F32, kind="ExternalOutput").ap()
    xn2_d = nc.dram_tensor("xn2", [T, D], F32, kind="ExternalOutput").ap()

    x1 = sb(nc, es, "x1_t", [128, NT * D])
    stat = sb(nc, es, "state", [128, 9 * 16])
    scr = sb(nc, es, "scre", [128, D])
    xnb16 = sb(nc, es, "xnb16", [128, D], BF16)
    xn2T = sb(nc, es, "xn2T", [128, 8 * T], BF16)
    wraw = sb(nc, es, "wraw", [128, 8 * 128], BF16)
    qb = sb(nc, es, "qb_t", [128, 8 * 512])
    kb = sb(nc, es, "kb_t", [128, 8 * 512])
    vb = sb(nc, es, "vb_t", [128, 8 * 512])
    ctxT = sb(nc, es, "ctxT", [128, 8 * 512], BF16)
    woT = sb(nc, es, "woT_t", [128, 8 * D], BF16)
    dkb = sb(nc, es, "dkb", [128, 512])
    m01 = sb(nc, es, "m01", [128, 512])
    prow = sb(nc, es, "prow", [2, 512])
    bicols = sb(nc, es, "bicols", [128, 24])
    bob = sb(nc, es, "bob", [128, D])
    seg2 = sb(nc, es, "seg2_t", [128, 2])
    seg2b = sb(nc, es, "seg2b_t", [2, 128])
    rowb = sb(nc, es, "rowb", [1, D])
    one1 = sb(nc, es, "one1e", [1, 128])
    iden = sb(nc, es, "iden_e", [128, 128])
    idenb = sb(nc, es, "idenb_e", [128, 128], BF16)

    ps1 = ps(nc, es, "ps1", [128, 512])
    ps2 = ps(nc, es, "ps2", [128, 512])
    psd = ps(nc, es, "psd", [128, 512])
    psb = [ps(nc, es, f"psb{j}", [128, 512]) for j in range(2)]
    psq = [ps(nc, es, f"psq{j}", [128, 1024], BF16) for j in range(2)]

    x13 = x1[:].rearrange("p (i d) -> p i d", d=D)
    xn2T3 = xn2T[:].rearrange("p (k t) -> p k t", t=T)
    qb3 = qb[:].rearrange("p (r c) -> p r c", c=512)
    kb3 = kb[:].rearrange("p (r c) -> p r c", c=512)
    vb3 = vb[:].rearrange("p (r c) -> p r c", c=512)
    ctxT3 = ctxT[:].rearrange("p (r c) -> p r c", c=512)
    woT3 = woT[:].rearrange("p (k n) -> p k n", n=D)
    wraw3 = wraw[:].rearrange("p (k m) -> p k m", m=128)
    DMA = lambda out, in_: (lambda s: s.dma_start(out, in_))

    t_x1 = p.op("sync", DMA(x13, x1_d.rearrange("(i p) d -> p i d", p=128)))
    t_idn = p.op("sync", DMA(iden[:], idn_d[:]))
    t_sg = p.op("sync", DMA(seg2[:], sg2_d[:]))
    t_sgb = p.op("sync", DMA(seg2b[:], sg2b_d[:]))
    t_bi = p.op("sync", DMA(rowb[:], bi_d[0:1, 0:D]))
    t_wo = p.op("sync", DMA(woT3, woT_d.rearrange("(k p) n -> p k n", p=128)))
    t_one = p.op("vector", lambda v: v.memset(one1[:], 1.0), serial=False)
    t_stm = p.op("vector", lambda v: v.memset(stat[:], 1.0), serial=False)
    t_idb = p.op("vector", lambda v: v.tensor_copy(idenb[:], iden[:]),
                 after=[t_idn])

    t_bic = None
    t_third = t_bi
    for third in range(3):
        if third > 0:
            t_third = p.op("sync", DMA(
                rowb[:], bi_d[0:1, D * third:D * (third + 1)]),
                after=[t_bic])
        tk = None
        for i in range(8):
            tk = p.op("tensor", lambda t, i=i, third=third: t.matmul(
                ps1[:, 8 * third + i:8 * third + i + 1],
                rowb[0:1, 128 * i:128 * (i + 1)], one1[0:1, 0:1],
                start=True, stop=True), serial=False,
                after=[t_one, t_third] + ([t_bic] if t_bic else [])
                if i == 0 else ())
        t_bic = p.op("scalar", lambda s, third=third: s.copy(
            bicols[:, 8 * third:8 * (third + 1)],
            ps1[:, 8 * third:8 * (third + 1)]), after=[tk])
    t_bo = p.op("sync", DMA(rowb[0:1, 0:D], bo_d[:]), after=[t_bic])
    ev = None
    for h in range(2):
        a = [t_one, t_bo, t_bic]
        if ev is not None:
            a.append(ev)
        tpe = p.op("tensor", lambda t, h=h: t.matmul(
            ps1[:, 0:512], one1[0:1, 0:128], rowb[0:1, 512 * h:512 * (h + 1)],
            start=True, stop=True), serial=False, after=a)
        ev = p.op("scalar", lambda s, h=h: s.copy(
            bob[:, 512 * h:512 * (h + 1)], ps1[:, 0:512]), after=[tpe])
    t_bob = ev

    tilesE = [(x13[:, i, :], 128, t_x1) for i in range(NT)]
    p.barrier_wait("scalar", [t_stm])
    M3, R3, tR3 = emit_ln_stats(p, tilesE, stat, scr)
    tp_toks = []
    ev_toks = []
    st_xn = []
    for i in range(NT):
        ta = p.op("vector", lambda v, i=i: v.tensor_scalar(
            x13[:, i, :], x13[:, i, :], M3[:, i:i + 1], R3[:, i:i + 1],
            ALU.subtract, ALU.mult), after=[tR3])
        st_xn.append(p.op("gdma", lambda g_, i=i: g_.dma_start(
            xn2_d[128 * i:128 * (i + 1), :], x13[:, i, :]), after=[ta]))
        a16 = [ta]
        if i >= 1:
            a16.append(tp_toks[-1])
        t16 = p.op("vector", lambda v, i=i: v.tensor_copy(
            xnb16[:], x13[:, i, :]), after=a16)
        for k in range(8):
            n_tp = len(tp_toks)
            j = n_tp % 2
            a = [t16, t_idb]
            if n_tp >= 2:
                a.append(ev_toks[n_tp - 2])
            tt = p.op("tensor", lambda t, k=k, j=j: t.transpose(
                psq[j][:, 0:128], xnb16[:, 128 * k:128 * (k + 1)], idenb[:]),
                serial=False, after=a)
            te = p.op("scalar", lambda s, k=k, j=j, i=i: s.copy(
                xn2T3[:, k, 128 * i:128 * (i + 1)], psq[j][:, 0:128]),
                after=[tt])
            tp_toks.append(tt)
            ev_toks.append(te)
    t_xn2T = ev_toks[-1]

    NLC = 4
    CL = LC // NLC
    st_s2 = []
    w_toks = {}
    route_toks = []
    for lc in range(NLC):
        l0 = lc * CL

        def cols(ap3, k, l0=l0):
            h2 = ap3[:, k, :].rearrange("p (b l) -> p b l", l=LC)
            return h2[:, :, l0:l0 + CL]

        for ft in range(24):
            gidx = lc * 24 + ft
            a = [t_xn2T, t_bic]
            if gidx >= 1:
                a.append(w_toks[gidx - 1])
            tlw = p.op("sync", lambda s, ft=ft: s.dma_start(
                wraw3, wiT_d[:, 128 * ft:128 * (ft + 1)].rearrange(
                    "(k p) m -> p k m", p=128)), after=a)
            pp = psb[gidx % 2]
            a2 = [tlw]
            if gidx >= 2:
                a2.append(route_toks[2 * (gidx - 1) - 1])
            if ft == 0 and lc >= 1:
                a2.append(ctx_last)
            tk = None
            for k in range(8):
                tk = p.op("tensor", lambda t, k=k, pp=pp, cols=cols: t.matmul(
                    pp[:, 0:512], wraw3[:, k, :], cols(xn2T3, k),
                    start=(k == 0), stop=(k == 7)), serial=False,
                    after=a2 if k == 0 else ())
            w_toks[gidx] = tk
            for s_ in (2 * ft, 2 * ft + 1):
                typ = s_ % 3
                head = s_ // 3
                r, hh = head // 2, head % 2
                dst = (qb3, kb3, vb3)[typ]
                lo = 64 * (s_ % 2)
                dlo = 64 * hh
                route_toks.append(p.op("scalar", lambda s, pp=pp, dst=dst, r=r,
                                  dlo=dlo, lo=lo, s_=s_: s.activation(
                                      dst[dlo:dlo + 64, r, :],
                                      pp[lo:lo + 64, 0:512], AF.Identity,
                                      bias=bicols[lo:lo + 64,
                                                  s_ // 2:s_ // 2 + 1]),
                                  after=[tk]))
        t_qkv = route_toks[-1]
        ctx_toks = []
        for r in range(8):
            tdk = p.op("vector", lambda v, r=r: v.tensor_tensor(
                dkb[:, 0:256], kb3[:, r, 0:256], kb3[:, r, 256:512],
                ALU.subtract), after=[t_qkv])
            tdv = p.op("vector", lambda v, r=r: v.tensor_tensor(
                dkb[:, 256:512], vb3[:, r, 0:256], vb3[:, r, 256:512],
                ALU.subtract))
            tm0 = p.op("vector", lambda v, r=r: v.tensor_tensor(
                m01[:, 0:256], qb3[:, r, 0:256], dkb[:, 0:256], ALU.mult))
            tm1 = p.op("vector", lambda v, r=r: v.tensor_tensor(
                m01[:, 256:512], qb3[:, r, 256:512], dkb[:, 0:256], ALU.mult))
            a = [tm1, t_sg]
            if r >= 1:
                a.append(ctx_toks[-1])
            td = p.op("tensor", lambda t: t.matmul(
                psd[0:2, 0:512], seg2[:], m01[:], start=True, stop=True),
                serial=False, after=a)
            tp_ = p.op("scalar", lambda s: s.activation(
                prow[:], psd[0:2, 0:512], AF.Sigmoid, scale=0.125),
                after=[td])
            for i_ in range(2):
                tb = p.op("tensor", lambda t, i_=i_: t.matmul(
                    psd[:, 0:256], seg2b[:], prow[0:2, 256 * i_:256 * (i_ + 1)],
                    start=True, stop=True), serial=False,
                    after=[tp_, t_sgb] + ([ctx_toks[-1]] if ctx_toks or i_ else []))
                tt1 = p.op("vector", lambda v, r=r: v.tensor_tensor(
                    m01[:, 0:256], psd[:, 0:256], dkb[:, 256:512], ALU.mult),
                    after=[tb, tdv])
                ctx_toks.append(p.op("vector", lambda v, r=r, i_=i_:
                                     v.tensor_tensor(
                                         ctxT3[:, r, 256 * i_:256 * (i_ + 1)],
                                         m01[:, 0:256], vb3[:, r, 256:512],
                                         ALU.add)))
        t_ctx = ctx_toks[-1]
        ctx_last = t_ctx
        for tb in range(4):
            pp1, pp2 = ps1, ps2
            a = [t_ctx, t_wo, t_bob]
            if len(st_s2) >= 1:
                a.append(st_s2[-1])
            tk1 = tk2 = None
            for k in range(8):
                tk1 = p.op("tensor", lambda t, k=k, tb=tb: t.matmul(
                    ps1[:, 0:512], ctxT3[:, k, 128 * tb:128 * (tb + 1)],
                    woT3[:, k, 0:512], start=(k == 0), stop=(k == 7)),
                    serial=False, after=a if k == 0 else ())
                tk2 = p.op("tensor", lambda t, k=k, tb=tb: t.matmul(
                    ps2[:, 0:512], ctxT3[:, k, 128 * tb:128 * (tb + 1)],
                    woT3[:, k, 512:1024], start=(k == 0), stop=(k == 7)),
                    serial=False)
            tv1 = p.op("vector", lambda v: v.tensor_tensor(
                scr[:, 0:512], ps1[:, 0:512], bob[:, 0:512], ALU.add),
                after=[tk1])
            tv2 = p.op("vector", lambda v: v.tensor_tensor(
                scr[:, 512:1024], ps2[:, 0:512], bob[:, 512:1024], ALU.add),
                after=[tk2])
            row0 = (tb // 2) * LC + l0 + 128 * (tb % 2)
            st_s2.append(p.op("gdma", lambda g_, row0=row0: g_.dma_start(
                s2_d[row0:row0 + 128, :], scr[:, 0:D]), after=[tv2]))

    p.barrier_wait("gpsimd", st_s2 + st_xn)
    p.finalize()
    return nc, es



def build_l1ffn(debug_stage=0):
    nc = bass.Bass("TRN2", target_bir_lowering=False, debug=False)
    es = contextlib.ExitStack()
    p = Prog(nc, es)

    xn2_d = nc.dram_tensor("xn2", [T, D], F32, kind="ExternalInput").ap()
    s2_d = nc.dram_tensor("s2", [T, D], F32, kind="ExternalInput").ap()
    x1_d = nc.dram_tensor("x1", [T, D], F32, kind="ExternalInput").ap()
    qwT_d = nc.dram_tensor("qwT", [6 * D, D], BF16, kind="ExternalInput").ap()
    qb_d = nc.dram_tensor("qb", [6, D], F32, kind="ExternalInput").ap()
    laT_d = nc.dram_tensor("laT", [6 * D, 32], BF16, kind="ExternalInput").ap()
    lbT_d = nc.dram_tensor("lbT", [6 * 32, D], BF16, kind="ExternalInput").ap()
    lng_d = nc.dram_tensor("lng", [3, D], F32, kind="ExternalInput").ap()
    lnb_d = nc.dram_tensor("lnb", [3, D], F32, kind="ExternalInput").ap()
    idn_d = nc.dram_tensor("iden", [128, 128], F32, kind="ExternalInput").ap()

    out_d = nc.dram_tensor("xout", [T, D], F32, kind="ExternalOutput").ap()

    h = sb(nc, es, "h_t", [128, NT * D])
    y = sb(nc, es, "y_t", [128, NT * D])
    hTb = sb(nc, es, "hTb", [128, 8 * T], BF16)
    stat = sb(nc, es, "statf", [128, 9 * 16])
    scr = sb(nc, es, "scrf", [128, D])
    hb16 = sb(nc, es, "hb16", [128, D], BF16)
    qwsl = sb(nc, es, "qwsl", [128, 8 * 128], BF16)
    lasl = sb(nc, es, "lasl", [128, 8 * 32], BF16)
    lbsl = sb(nc, es, "lbsl", [32, D], BF16)
    t1T = sb(nc, es, "t1T", [32, T], BF16)
    qbb = sb(nc, es, "qbb", [128, D])
    gbb = sb(nc, es, "gbb", [128, D])
    bbb = sb(nc, es, "bbb", [128, D])
    rowb = sb(nc, es, "rowbf", [1, D])
    one1 = sb(nc, es, "one1f", [1, 128])
    iden = sb(nc, es, "iden_f", [128, 128])
    idenb = sb(nc, es, "idenb_f", [128, 128], BF16)
    stl = sb(nc, es, "stlf", [128, D])
    xtl = sb(nc, es, "xtlf", [128, D])

    ps1 = ps(nc, es, "ps1", [128, 512])
    ps2 = ps(nc, es, "ps2", [128, 512])
    pst = ps(nc, es, "pst", [128, 512])
    psq = [ps(nc, es, f"psq{j}", [128, 1024], BF16) for j in range(2)]

    h3 = h[:].rearrange("p (i d) -> p i d", d=D)
    y3 = y[:].rearrange("p (i d) -> p i d", d=D)
    hTb3 = hTb[:].rearrange("p (k t) -> p k t", t=T)
    qwsl3 = qwsl[:].rearrange("p (k m) -> p k m", m=128)
    lasl3 = lasl[:].rearrange("p (k r) -> p k r", r=32)
    DMA = lambda out, in_: (lambda s: s.dma_start(out, in_))

    t_h = p.op("sync", DMA(h3, xn2_d.rearrange("(i p) d -> p i d", p=128)))
    t_idn = p.op("sync", DMA(iden[:], idn_d[:]))
    t_one = p.op("vector", lambda v: v.memset(one1[:], 1.0), serial=False)
    t_stm = p.op("vector", lambda v: v.memset(stat[:], 1.0), serial=False)
    t_idb = p.op("vector", lambda v: v.tensor_copy(idenb[:], iden[:]),
                 after=[t_idn])

    def row_bcast(dst, row_tok, extra=()):
        ev = None
        for hh in range(2):
            a = [t_one, row_tok] + list(extra)
            if ev is not None:
                a.append(ev)
            tpe = p.op("tensor", lambda t, hh=hh: t.matmul(
                pst[:, 0:512], one1[0:1, 0:128],
                rowb[0:1, 512 * hh:512 * (hh + 1)], start=True, stop=True),
                serial=False, after=a)
            ev = p.op("scalar", lambda s, hh=hh, dst=dst: s.copy(
                dst[:, 512 * hh:512 * (hh + 1)], pst[:, 0:512]), after=[tpe])
        return ev

    rowb_free = [None]

    tp_hist = []

    def transpose_tile(i, src16_tok, tag_dst):
        te = None
        for k in range(8):
            n_tp = len(tp_hist)
            j = n_tp % 2
            a = [src16_tok, t_idb]
            if n_tp >= 2:
                a.append(tp_hist[n_tp - 2][1])
            tt = p.op("tensor", lambda t, k=k, j=j: t.transpose(
                psq[j][:, 0:128], hb16[:, 128 * k:128 * (k + 1)], idenb[:]),
                serial=False, after=a)
            te = p.op("scalar", lambda s, k=k, j=j, i=i: s.copy(
                hTb3[:, k, 128 * i:128 * (i + 1)], psq[j][:, 0:128]),
                after=[tt])
            tp_hist.append((tt, te))
        return te

    def transpose_src_to_hTb(src3, toks):
        out_toks = []
        for i in range(NT):
            a16 = [toks[i] if isinstance(toks, list) else toks]
            if tp_hist:
                a16.append(tp_hist[-1][0])
            t16 = p.op("vector", lambda v, i=i, src3=src3: v.tensor_copy(
                hb16[:], src3[:, i, :]), after=a16)
            out_toks.append(transpose_tile(i, t16, None))
        return out_toks

    ht_toks = transpose_src_to_hTb(h3, t_h)

    def emit_qlora(widx, src_hT_toks, dst3, dst_extra):
        a0 = [rowb_free[0]] if rowb_free[0] else []
        tlq = p.op("sync", DMA(rowb[:], qb_d[widx:widx + 1, :]), after=a0)
        t_qbb = row_bcast(qbb, tlq, extra=a0)
        rowb_free[0] = t_qbb
        ev_toks = []
        t1_toks = {}
        tla = p.op("sync", DMA(lasl3, laT_d[
            widx * D:(widx + 1) * D, :].rearrange("(k p) r -> p k r", p=128)),
            after=[src_hT_toks[-1]])
        tlb = p.op("sync", DMA(lbsl[:], lbT_d[widx * 32:(widx + 1) * 32, :]))
        for i in range(NT):
            a = [tla, src_hT_toks[-1], t_qbb]
            if i >= 1:
                a.append(t1_toks[i - 1])
            tk = None
            for k in range(8):
                tk = p.op("tensor", lambda t, k=k, i=i: t.matmul(
                    pst[:, 0:32], hTb3[:, k, 128 * i:128 * (i + 1)],
                    lasl3[:, k, :], start=(k == 0), stop=(k == 7)),
                    serial=False, after=a if k == 0 else ())
            n_tp = len(tp_hist)
            j = n_tp % 2
            tcp = p.op("vector", lambda v, i=i: v.tensor_copy(
                hb16[:, 0:32], pst[:, 0:32]), after=[tk])
            a2 = [tcp, t_idb]
            if n_tp >= 2:
                a2.append(tp_hist[n_tp - 2][1])
            tt = p.op("tensor", lambda t, j=j: t.transpose(
                psq[j][0:32, 0:128], hb16[:, 0:32], idenb[:]),
                serial=False, after=a2)
            t1_toks[i] = p.op("scalar", lambda s, j=j, i=i: s.copy(
                t1T[0:32, 128 * i:128 * (i + 1)], psq[j][0:32, 0:128]),
                after=[tt])
            tp_hist.append((tt, t1_toks[i]))
        t_t1 = t1_toks[NT - 1]
        wq_toks = {}
        for ot in range(8):
            a = [src_hT_toks[-1]]
            if ot >= 1:
                a.append(wq_toks[ot - 1][1])
            tlw = p.op("sync", lambda s, ot=ot, widx=widx: s.dma_start(
                qwsl3, qwT_d[widx * D:(widx + 1) * D,
                             128 * ot:128 * (ot + 1)].rearrange(
                                 "(k p) m -> p k m", p=128)), after=a)
            mm_last = None
            for i in range(NT):
                pp = ps1 if (i % 2 == 0) else ps2
                a2 = [tlw, t_t1, tlb]
                if len(ev_toks) >= 2:
                    a2.append(ev_toks[-2])
                tk = None
                for k in range(8):
                    tk = p.op("tensor", lambda t, k=k, i=i, pp=pp: t.matmul(
                        pp[:, 0:128], hTb3[:, k, 128 * i:128 * (i + 1)],
                        qwsl3[:, k, :], start=(k == 0), stop=False),
                        serial=False, after=a2 if k == 0 else ())
                tk = p.op("tensor", lambda t, i=i, pp=pp, ot=ot: t.matmul(
                    pp[:, 0:128], t1T[0:32, 128 * i:128 * (i + 1)],
                    lbsl[0:32, 128 * ot:128 * (ot + 1)],
                    start=False, stop=True), serial=False)
                mm_last = tk
                ev_toks.append(p.op("vector", lambda v, i=i, ot=ot, pp=pp,
                                    widx=widx, dst3=dst3: v.tensor_tensor(
                                        dst3[:, i, 128 * ot:128 * (ot + 1)],
                                        pp[:, 0:128],
                                        qbb[:, 128 * ot:128 * (ot + 1)],
                                        ALU.add), after=[tk, t_qbb] + dst_extra))
            wq_toks[ot] = (tlw, mm_last)
        return ev_toks

    st_out = []
    for blk in range(3):
        ev1 = emit_qlora(2 * blk, ht_toks, y3, [])
        rowb_free[0] = ev1[-1]
        if debug_stage == 1:
            for i in range(NT):
                st_out.append(p.op("gdma", lambda g_, i=i: g_.dma_start(
                    out_d[128 * i:128 * (i + 1), :], y3[:, i, :]),
                    after=[ev1[-1]]))
            p.barrier_wait("gpsimd", st_out)
            p.finalize()
            return nc, es
        tilesF = [(y3[:, i, :], 128, ev1[-1]) for i in range(NT)]
        p.barrier_wait("scalar", [t_stm])
        Mf, Rf, tRf = emit_ln_stats(p, tilesF, stat, scr)
        tlg = p.op("sync", DMA(rowb[:], lng_d[blk:blk + 1, :]),
                   after=[tRf, rowb_free[0]])
        t_gb = row_bcast(gbb, tlg)
        tlb2 = p.op("sync", DMA(rowb[:], lnb_d[blk:blk + 1, :]), after=[t_gb])
        t_bb = row_bcast(bbb, tlb2, extra=[t_gb])
        rowb_free[0] = t_bb
        ap_toks = []
        for i in range(NT):
            p.op("vector", lambda v, i=i: v.tensor_scalar(
                y3[:, i, :], y3[:, i, :], Mf[:, i:i + 1], Rf[:, i:i + 1],
                ALU.subtract, ALU.mult), after=[tRf, t_gb, t_bb])
            p.op("vector", lambda v, i=i: v.tensor_tensor(
                y3[:, i, :], y3[:, i, :], gbb[:], ALU.mult))
            tv = p.op("vector", lambda v, i=i: v.tensor_tensor(
                y3[:, i, :], y3[:, i, :], bbb[:], ALU.add))
            ap_toks.append(p.op("scalar", lambda s, i=i: s.activation(
                y3[:, i, :], y3[:, i, :], AF.Gelu), after=[tv]))
        yt_toks = transpose_src_to_hTb(y3, ap_toks)
        ev2 = emit_qlora(2 * blk + 1, yt_toks, y3, [yt_toks[-1]])
        rowb_free[0] = ev2[-1]
        add_toks = []
        for i in range(NT):
            add_toks.append(p.op("vector", lambda v, i=i: v.tensor_tensor(
                h3[:, i, :], h3[:, i, :], y3[:, i, :], ALU.add),
                after=[ev2[-1]]))
        if blk < 2:
            ht_toks = transpose_src_to_hTb(h3, add_toks)
        else:
            for i in range(NT):
                tsi = p.op("scalar", lambda s, i=i: s.activation(
                    scr[:], h3[:, i, :], AF.Silu), after=[add_toks[i]] +
                    ([st_out[-1]] if st_out else []))
                tls = p.op("sync", DMA(stl[:], s2_d[128 * i:128 * (i + 1), :]),
                           after=[st_out[-1]] if st_out else [tsi])
                tlx = p.op("sync", DMA(xtl[:], x1_d[128 * i:128 * (i + 1), :]))
                tv1 = p.op("vector", lambda v: v.tensor_tensor(
                    stl[:], stl[:], scr[:], ALU.mult), after=[tsi, tls])
                tv2 = p.op("vector", lambda v: v.tensor_tensor(
                    stl[:], stl[:], xtl[:], ALU.add), after=[tlx])
                st_out.append(p.op("gdma", lambda g_, i=i: g_.dma_start(
                    out_d[128 * i:128 * (i + 1), :], stl[:]), after=[tv2]))

    p.barrier_wait("gpsimd", st_out)
    p.finalize()
    return nc, es



def _mk_iden():
    return np.eye(128, dtype=np.float32)


def mk_sel():
    sel = np.zeros((32, 32 * 128), np.float32)
    for s in range(32):
        sel[s, 128 * s:128 * (s + 1)] = 1.0
    return sel


def shard_x(x):
    out = []
    for c in range(NCORES):
        sl = slice(c * LC, (c + 1) * LC)
        xc = np.ascontiguousarray(
            np.concatenate([x[0, sl], x[1, sl]], axis=0).astype(np.float32))
        if c == 0:
            halo = np.zeros((8, D), np.float32)
            hmask = np.zeros((128, 1), np.float32)
        else:
            halo = np.concatenate(
                [x[0, c * LC - 3:c * LC], x[1, c * LC - 3:c * LC],
                 np.zeros((2, D), np.float32)], axis=0).astype(np.float32)
            hmask = np.ones((128, 1), np.float32)
        out.append((xc, np.ascontiguousarray(halo), hmask))
    return out


def run_l0pre(inputs, x_full):
    nc, es = build_l0pre()
    iden = _mk_iden()
    shards = shard_x(x_full)
    in_maps = []
    for c in range(NCORES):
        xc, halo, hmask = shards[c]
        in_maps.append({
            "x": xc, "xhalo": halo, "hmask": hmask,
            "ln1_g": np.ascontiguousarray(inputs["L0_attn_ln1_g"].reshape(1, D)),
            "ln1_b": np.ascontiguousarray(inputs["L0_attn_ln1_b"].reshape(1, D)),
            "in_proj_w": np.ascontiguousarray(inputs["L0_attn_in_proj_w"]),
            "conv_w": np.ascontiguousarray(
                inputs["L0_attn_conv_w"].reshape(DIN, 4)),
            "conv_b": np.ascontiguousarray(
                inputs["L0_attn_conv_b"].reshape(1, DIN)),
            "x_proj_w": np.ascontiguousarray(inputs["L0_attn_x_proj_w"]),
            "dt_proj_w": np.ascontiguousarray(inputs["L0_attn_dt_proj_w"]),
            "dt_proj_b": np.ascontiguousarray(
                inputs["L0_attn_dt_proj_b"].reshape(1, DIN)),
            "iden": iden,
        })
    res = run_bass_kernel_spmd(nc, in_maps, core_ids=list(range(NCORES)))
    es.close()
    return res.results


def run_scan(dtT_full, xcT_full, BC_full, A_log):
    nc, es = build_scan()
    sel = mk_sel()
    in_maps = []
    for c in range(NCORES):
        r = slice(c * DSH, (c + 1) * DSH)
        in_maps.append({
            "dts": np.ascontiguousarray(dtT_full[r]),
            "xcs": np.ascontiguousarray(xcT_full[r]),
            "BCs": np.ascontiguousarray(BC_full),
            "Als": np.ascontiguousarray(A_log[r]),
            "sel": sel,
        })
    res = run_bass_kernel_spmd(nc, in_maps, core_ids=list(range(NCORES)))
    es.close()
    return np.concatenate([res.results[c]["ys"] for c in range(NCORES)], axis=0)


def run_l0s(inputs, xn_full, yT, xcT, zT):
    nc, es = build_l0s()
    iden = _mk_iden()
    in_maps = []
    for c in range(NCORES):
        in_maps.append({
            "yT": np.ascontiguousarray(yT[c]),
            "xcT": np.ascontiguousarray(xcT[c]),
            "zT": np.ascontiguousarray(zT[c]),
            "xn": np.ascontiguousarray(xn_full[c]),
            "Dv": np.ascontiguousarray(inputs["L0_attn_D"].reshape(1, DIN)),
            "opw": np.ascontiguousarray(inputs["L0_attn_out_proj_w"]),
            "ln2_g": np.ascontiguousarray(inputs["L0_attn_ln2_g"].reshape(1, D)),
            "ln2_b": np.ascontiguousarray(inputs["L0_attn_ln2_b"].reshape(1, D)),
            "ffn_w1": np.ascontiguousarray(inputs["L0_attn_ffn_w1"]),
            "ffn_b1": np.ascontiguousarray(inputs["L0_attn_ffn_b1"].reshape(1, DFF)),
            "ffn_w2": np.ascontiguousarray(inputs["L0_attn_ffn_w2"]),
            "ffn_b2": np.ascontiguousarray(inputs["L0_attn_ffn_b2"].reshape(1, D)),
            "iden": iden,
        })
    res = run_bass_kernel_spmd(nc, in_maps, core_ids=list(range(NCORES)))
    es.close()
    return [res.results[c]["s_out"] for c in range(NCORES)]


def run_l0g(prepw, xn_pc, s_pc, x_pc):
    nc, es = build_l0g()
    import ml_dtypes
    idenb = np.eye(128).astype(ml_dtypes.bfloat16)
    in_maps = []
    for c in range(NCORES):
        in_maps.append({
            "xn": np.ascontiguousarray(xn_pc[c]),
            "s_in": np.ascontiguousarray(s_pc[c]),
            "x": np.ascontiguousarray(x_pc[c]),
            "wgT": np.ascontiguousarray(prepw["wgT"]),
            "wuT": np.ascontiguousarray(prepw["wuT"]),
            "wdT": np.ascontiguousarray(prepw["wdT"]),
            "idenb": idenb,
        })
    res = run_bass_kernel_spmd(nc, in_maps, core_ids=list(range(NCORES)))
    es.close()
    return [res.results[c]["x1"] for c in range(NCORES)]


def run_prep(inputs):
    nc, es = build_prep()
    import ml_dtypes
    idenb = np.eye(128).astype(ml_dtypes.bfloat16)
    qw_all = np.ascontiguousarray(inputs["L1_ffn_qweights"].reshape(6 * D, D))
    sc_all = np.ascontiguousarray(inputs["L1_ffn_scales"].reshape(6 * D, 64))
    in_maps = []
    for c in range(NCORES):
        la = np.zeros((32, D), np.float32)
        lb = np.zeros((D, 32), np.float32)
        if c < 6:
            la = np.ascontiguousarray(inputs["L1_ffn_lora_a"][c])
            lb = np.ascontiguousarray(inputs["L1_ffn_lora_b"][c])
        in_maps.append({
            "wg": np.ascontiguousarray(
                inputs["L0_ffn_w_gate"][FP4_SHARE * c:FP4_SHARE * (c + 1)]),
            "wu": np.ascontiguousarray(
                inputs["L0_ffn_w_up"][FP4_SHARE * c:FP4_SHARE * (c + 1)]),
            "wd": np.ascontiguousarray(
                inputs["L0_ffn_w_down"][WD_SHARE * c:WD_SHARE * (c + 1)]),
            "qw": np.ascontiguousarray(qw_all[QW_SHARE * c:QW_SHARE * (c + 1)]),
            "qsc": np.ascontiguousarray(sc_all[QW_SHARE * c:QW_SHARE * (c + 1)]),
            "win": np.ascontiguousarray(
                inputs["L1_attn_in_proj_weight"][WIN_SHARE * c:WIN_SHARE * (c + 1)]),
            "wout": np.ascontiguousarray(
                inputs["L1_attn_out_proj_weight"][128 * c:128 * (c + 1)]),
            "la": la, "lb": lb, "idenb": idenb,
        })
    res = run_bass_kernel_spmd(nc, in_maps, core_ids=list(range(NCORES)))
    es.close()
    r = [res.results[c] for c in range(NCORES)]
    out = {}
    out["wgT"] = np.concatenate([r[c]["wgT"] for c in range(NCORES)], axis=1)
    out["wuT"] = np.concatenate([r[c]["wuT"] for c in range(NCORES)], axis=1)
    out["wdT"] = np.concatenate([r[c]["wdT"] for c in range(NCORES)], axis=1)
    qwT = np.concatenate([r[c]["qwT"] for c in range(NCORES)], axis=1)
    out["qwT"] = np.ascontiguousarray(
        qwT.reshape(D, 6, D).transpose(1, 0, 2))
    out["winT"] = np.concatenate([r[c]["winT"] for c in range(NCORES)], axis=1)
    out["woutT"] = np.concatenate([r[c]["woutT"] for c in range(NCORES)], axis=1)
    out["laT"] = np.stack([r[c]["laT"] for c in range(6)])
    out["lbT"] = np.stack([r[c]["lbT"] for c in range(6)])
    return out


def run_l1att(prepw, inputs, x1_pc):
    nc, es = build_l1att()
    iden = _mk_iden()
    seg2 = np.zeros((128, 2), np.float32)
    seg2[0:64, 0] = 1.0
    seg2[64:128, 1] = 1.0
    seg2b = np.ascontiguousarray(seg2.T)
    in_maps = []
    for c in range(NCORES):
        in_maps.append({
            "x1": np.ascontiguousarray(x1_pc[c]),
            "winT": np.ascontiguousarray(prepw["winT"]),
            "in_bias": np.ascontiguousarray(
                inputs["L1_attn_in_proj_bias"].reshape(1, 3 * D)),
            "woutT": np.ascontiguousarray(prepw["woutT"]),
            "out_bias": np.ascontiguousarray(
                inputs["L1_attn_out_proj_bias"].reshape(1, D)),
            "seg2": seg2, "seg2b": seg2b, "iden": iden,
        })
    res = run_bass_kernel_spmd(nc, in_maps, core_ids=list(range(NCORES)))
    es.close()
    return ([res.results[c]["s2"] for c in range(NCORES)],
            [res.results[c]["xn2"] for c in range(NCORES)])


def run_l1ffn(prepw, inputs, xn2_pc, s2_pc, x1_pc):
    nc, es = build_l1ffn()
    iden = _mk_iden()
    qwT = np.ascontiguousarray(np.asarray(prepw["qwT"]).reshape(6 * D, D))
    laT = np.ascontiguousarray(np.asarray(prepw["laT"]).reshape(6 * D, 32))
    lbT = np.ascontiguousarray(np.asarray(prepw["lbT"]).reshape(6 * 32, D))
    in_maps = []
    for c in range(NCORES):
        in_maps.append({
            "xn2": np.ascontiguousarray(xn2_pc[c]),
            "s2": np.ascontiguousarray(s2_pc[c]),
            "x1": np.ascontiguousarray(x1_pc[c]),
            "qwT": qwT,
            "qb": np.ascontiguousarray(inputs["L1_ffn_biases"]),
            "laT": laT, "lbT": lbT,
            "lng": np.ascontiguousarray(inputs["L1_ffn_ln_gamma"]),
            "lnb": np.ascontiguousarray(inputs["L1_ffn_ln_beta"]),
            "iden": iden,
        })
    res = run_bass_kernel_spmd(nc, in_maps, core_ids=list(range(NCORES)))
    es.close()
    return [res.results[c]["xout"] for c in range(NCORES)]


def kernel(**inputs):
    x = np.asarray(inputs["L0_attn_x"], dtype=np.float32)
    prepw = run_prep(inputs)
    resB = run_l0pre(inputs, x)
    dtT_full = np.empty((DIN, B * L), np.float32)
    xcT_full = np.empty((DIN, B * L), np.float32)
    bc_full = np.empty((32, B * L), np.float32)
    for c in range(NCORES):
        for b in range(B):
            dst = slice(b * L + c * LC, b * L + (c + 1) * LC)
            srcc = slice(b * LC, (b + 1) * LC)
            dtT_full[:, dst] = resB[c]["dtT"][:, srcc]
            xcT_full[:, dst] = resB[c]["xcT"][:, srcc]
            bc_full[:, dst] = resB[c]["BC"][:, srcc]
    y_full = run_scan(dtT_full, xcT_full, bc_full,
                      np.ascontiguousarray(inputs["L0_attn_A_log"]))
    yT_pc = []
    for c in range(NCORES):
        yT_pc.append(np.concatenate(
            [y_full[:, c * LC:(c + 1) * LC],
             y_full[:, L + c * LC:L + (c + 1) * LC]], axis=1))
    xn_pc = [resB[c]["xn"] for c in range(NCORES)]
    xcT_pc = [resB[c]["xcT"] for c in range(NCORES)]
    zT_pc = [resB[c]["zT"] for c in range(NCORES)]
    s_pc = run_l0s(inputs, xn_pc, yT_pc, xcT_pc, zT_pc)
    x_pc = [np.ascontiguousarray(np.concatenate(
        [x[0, c * LC:(c + 1) * LC], x[1, c * LC:(c + 1) * LC]]))
        for c in range(NCORES)]
    x1_pc = run_l0g(prepw, xn_pc, s_pc, x_pc)
    s2_pc, xn2_pc = run_l1att(prepw, inputs, x1_pc)
    xout_pc = run_l1ffn(prepw, inputs, xn2_pc, s2_pc, x1_pc)
    out = np.empty((B, L, D), np.float32)
    for c in range(NCORES):
        out[0, c * LC:(c + 1) * LC] = xout_pc[c][0:LC]
        out[1, c * LC:(c + 1) * LC] = xout_pc[c][LC:T]
    return out
